# version 04
# baseline (speedup 1.0000x reference)
import contextlib
import numpy as np
import concourse.bass as bass
import concourse.mybir as mybir
from concourse.bass_utils import run_bass_kernel_spmd

F32 = mybir.dt.float32
BF16 = mybir.dt.bfloat16
AF = mybir.ActivationFunctionType
OP = mybir.AluOpType

P = 128
NCC = 16
TT = 512
NT = 5
TREAL = 2048
HALO = 512
NH = 64
FBT = 256
LN_EPS = 1e-5
FWHT_SCALE = float(np.float32(1.0 / np.sqrt(np.float64(8192.0))))


class Sem:
    def __init__(self, handle, name):
        self.h = handle
        self.name = name
        self.n = 0

    def inc(self, k=1):
        self.n += k
        return self.n


class Eng:
    def __init__(self, name):
        self.name = name
        self.steps = []
        self.lw = {}

    def wait(self, sem, val):
        if val <= 0 or self.lw.get(sem.name, 0) >= val:
            return
        self.lw[sem.name] = val
        self.steps.append(lambda e, s=sem, v=val: e.wait_ge(s.h, v))

    def run(self, fn):
        self.steps.append(fn)


def build_program():
    nc = bass.Bass("TRN2", target_bir_lowering=False)

    xT_h = nc.dram_tensor("xT", [2048, 2560], F32, kind="ExternalInput").ap()
    vec_h = nc.dram_tensor("vecs", [P, 7, NCC], F32, kind="ExternalInput").ap()
    pm_h = nc.dram_tensor("pmask", [P, 1], F32, kind="ExternalInput").ap()
    ones_h = nc.dram_tensor("ones", [P, 1], F32, kind="ExternalInput").ap()
    onesr_h = nc.dram_tensor("onesr", [1, P], F32, kind="ExternalInput").ap()
    h128_h = nc.dram_tensor("h128", [P, P], F32, kind="ExternalInput").ap()
    h64_h = nc.dram_tensor("h64", [64, 64], F32, kind="ExternalInput").ap()
    wk_h = nc.dram_tensor("wk", [NCC, P, NCC, P], F32, kind="ExternalInput").ap()
    wv_h = nc.dram_tensor("wv", [NCC, P, NCC, P], F32, kind="ExternalInput").ap()
    wr_h = nc.dram_tensor("wr", [NCC, P, NCC, P], F32, kind="ExternalInput").ap()
    wo_h = nc.dram_tensor("wo", [NCC, P, NCC, P], F32, kind="ExternalInput").ap()
    w1_h = nc.dram_tensor("w1", [NH, P, NCC, P], F32, kind="ExternalInput").ap()
    w3_h = nc.dram_tensor("w3", [NH, P, NCC, P], F32, kind="ExternalInput").ap()
    w2_h = nc.dram_tensor("w2", [NH, P, NCC, P], F32, kind="ExternalInput").ap()
    out_h = nc.dram_tensor("outT", [2048, 2048], F32, kind="ExternalOutput").ap()

    wq_sc = nc.dram_tensor("wq_sc", [3, NCC, P, NCC * P], BF16, kind="Internal").ap()
    xnb_sc = nc.dram_tensor("xnb_sc", [4, NCC, P, TT], BF16, kind="Internal").ap()
    rwkv_sc = nc.dram_tensor("rwkv_sc", [4, NCC, P, TT], BF16, kind="Internal").ap()
    ao_sc = nc.dram_tensor("ao_sc", [4, NCC, P, TT], BF16, kind="Internal").ap()
    mrs_sc = nc.dram_tensor("mrs_sc", [4, 2, TT], F32, kind="Internal").ap()
    w2a_sc = nc.dram_tensor("w2a_sc", [NH, P, 2048], BF16, kind="Internal").ap()
    w2p_sc = nc.dram_tensor("w2p_sc", [NH, P, 2048], BF16, kind="Internal").ap()
    hid_sc = nc.dram_tensor("hid_sc", [NH, 8, P, FBT], BF16, kind="Internal").ap()

    with contextlib.ExitStack() as es:
        ec = es.enter_context
        A = ec(nc.sbuf_tensor("A", [P, 16 * TT], F32))
        B = ec(nc.sbuf_tensor("B", [P, 16 * TT], F32))
        C = ec(nc.sbuf_tensor("C", [P, 3 * 16 * TT], BF16))
        G = ec(nc.sbuf_tensor("G", [P, 16384], BF16))
        H = ec(nc.sbuf_tensor("H", [P, 18 * TT], F32))
        E = ec(nc.sbuf_tensor("E", [P, 4096], BF16))
        S = ec(nc.sbuf_tensor("S", [P, 1024], F32))
        ABu = ec(nc.sbuf_tensor("ABu", [P, TT + 1], F32))
        BBu = ec(nc.sbuf_tensor("BBu", [P, TT + 1], F32))
        SR = ec(nc.sbuf_tensor("SR", [1, 1024], F32))
        VEC = ec(nc.sbuf_tensor("VEC", [P, 7, NCC], F32))
        DER = ec(nc.sbuf_tensor("DER", [P, 64], F32))
        PM = ec(nc.sbuf_tensor("PMt", [P, 1], F32))
        ONES = ec(nc.sbuf_tensor("ONESt", [P, 1], F32))
        ONESR = ec(nc.sbuf_tensor("ONESRt", [1, P], F32))
        H128B = ec(nc.sbuf_tensor("H128Bt", [P, P], BF16))
        H64B = ec(nc.sbuf_tensor("H64Bt", [64, 64], BF16))
        CA = ec(nc.sbuf_tensor("CAt", [P, NCC], F32))
        CB = ec(nc.sbuf_tensor("CBt", [P, NCC], F32))
        CX = ec(nc.sbuf_tensor("CXt", [P, NCC], F32))
        PSB = [ec(nc.psum_tensor(f"PSB{i}", [P, TT], F32)) for i in range(8)]

        s_x = Sem(ec(nc.semaphore("s_x")), "s_x")
        s_g = Sem(ec(nc.semaphore("s_g")), "s_g")
        s_sp = Sem(ec(nc.semaphore("s_sp")), "s_sp")
        s_t = Sem(ec(nc.semaphore("s_t")), "s_t")
        s_v = Sem(ec(nc.semaphore("s_v")), "s_v")
        s_a = Sem(ec(nc.semaphore("s_a")), "s_a")
        block = ec(nc.Block())

        SY = Eng("sync"); GP = Eng("gpsimd"); SC = Eng("scalar")
        VE = Eng("vector"); TE = Eng("tensor")
        ALL_ENG = (SY, GP, SC, VE, TE)

        def barrier(*engs):
            for eng in engs:
                for sem in (s_t, s_v, s_a, s_x, s_g, s_sp):
                    eng.wait(sem, sem.n)

        Af = A.ap().rearrange("p (c t) -> p c t", c=16)
        Bf = B.ap().rearrange("p (c t) -> p c t", c=16)
        Abf = A.ap().bitcast(BF16)
        Bbf = B.ap().bitcast(BF16)
        Hbf = H.ap().bitcast(BF16)
        Cm = C.ap().rearrange("p (q c t) -> p q c t", q=3, c=16)
        Cht = C.ap().rearrange("p (h t) -> p h t", t=FBT)

        def Hs(i):
            return H.ap()[:, i * TT:(i + 1) * TT]

        def Es(i):
            return E.ap()[:, i * TT:(i + 1) * TT]

        def Gs(i):
            return G.ap()[:, i * 2048:(i + 1) * 2048]

        EWc = DER.ap()[:, 16:32]
        EUc = DER.ap()[:, 32:48]

        def vslice(j, cc):
            return VEC.ap()[:, j, cc:cc + 1]

        def dma(ENG, sem, out, in_):
            ENG.run(lambda e, o=out, i=in_, s=sem: e.dma_start(out=o, in_=i).then_inc(s.h, 16))
            return sem.inc(16)

        def mm_acc(bank, lhsT_list, rhs_list, waits=()):
            n = len(lhsT_list)
            for sem, val in waits:
                TE.wait(sem, val)
            for i in range(n):
                last = i == n - 1
                lt, rh = lhsT_list[i], rhs_list[i]
                if last:
                    TE.run(lambda e, lt=lt, rh=rh, b=bank, st=(i == 0): e.matmul(
                        out=PSB[b].ap(), lhsT=lt, rhs=rh, start=st,
                        stop=True).then_inc(s_t.h, 1))
                else:
                    TE.run(lambda e, lt=lt, rh=rh, b=bank, st=(i == 0): e.matmul(
                        out=PSB[b].ap(), lhsT=lt, rhs=rh, start=st, stop=False))
            return s_t.inc()

        evac_a = {}
        psum_v = {}
        sq_mm = {}
        gslot_t = {}
        gslot_v = {}
        eslot_sp = {}

        n_consts = 0
        for out, in_ in ((VEC.ap(), vec_h), (PM.ap(), pm_h), (ONES.ap(), ones_h),
                         (ONESR.ap(), onesr_h)):
            n_consts = dma(SY, s_x, out, in_)
        n_h = 0
        for out, in_ in ((H128B.ap(), h128_h), (H64B.ap(), h64_h)):
            n_h = dma(GP, s_g, out, in_)

        SC.wait(s_x, n_consts)
        SC.run(lambda e: e.activation(out=DER.ap()[:, 0:16], in_=VEC.ap()[:, 6, :],
                                      func=AF.Exp).then_inc(s_a.h, 1))
        a0 = s_a.inc()
        VE.wait(s_x, n_consts)
        VE.wait(s_a, a0)
        VE.run(lambda e: e.tensor_single_scalar(out=DER.ap()[:, 0:16], in_=DER.ap()[:, 0:16],
                                                scalar=-1.0, op=OP.mult).then_inc(s_v.h, 1))
        v0 = s_v.inc()
        SC.wait(s_v, v0)
        SC.run(lambda e: e.activation(out=DER.ap()[:, 16:32], in_=DER.ap()[:, 0:16], func=AF.Exp))
        SC.run(lambda e: e.activation(out=DER.ap()[:, 32:48], in_=VEC.ap()[:, 5, :],
                                      func=AF.Exp).then_inc(s_a.h, 1))
        a_der = s_a.inc()
        VE.run(lambda e: e.memset(CA.ap(), 0.0))
        VE.run(lambda e: e.memset(CB.ap(), 0.0))
        VE.run(lambda e: e.memset(CX.ap(), 0.0).then_inc(s_v.h, 1))
        s_v.inc()

        wqh = (wk_h, wv_h, wr_h)
        cast_slot_store = {6: 0, 7: 0}
        cast_ready = {}
        for q in range(3):
            for mc in range(NCC):
                slot = 6 + (q * NCC + mc) % 2
                GP.wait(s_g, cast_slot_store[slot])
                g_ld = dma(GP, s_g, Gs(slot), wqh[q][mc].rearrange("p c q -> p (c q)"))
                GP.wait(s_g, g_ld)
                cast_slot_store[slot] = dma(GP, s_g, wq_sc[q, mc], Gs(slot))
                cast_ready[(q, mc)] = cast_slot_store[slot]
        g_castpass = s_g.n

        SQSL = (0, 1)
        KF = (2, 3); VF = (4, 5); RF = (6, 7)
        EKs, EKVs, T1s, T2s, NUMs, DENs, RECs = 8, 9, 10, 11, 12, 13, 14
        XN2s, PREVs, TMPs = 15, 16, 17

        mrow = SR.ap()[:, 0:TT]; rrow = SR.ap()[:, TT:2 * TT]

        last_norm_v = 0
        last_stat_v = 0
        mrs_spill = 0
        v_mixes = 0
        kv_mm_done = {}
        scan_v = {}
        scan_v_prev = 0

        for tau in range(NT):
            real = tau >= 1
            r = tau - 1

            xre = xT_h.rearrange("(c p) t -> p c t", p=P)
            SY.wait(s_v, last_norm_v)
            x_ld = 0
            for ccg in range(4):
                x_ld = dma(SY, s_x, Af[:, ccg * 4:(ccg + 1) * 4, :],
                           xre[:, ccg * 4:(ccg + 1) * 4, TT * tau: TT * (tau + 1)])

            ln_mm = 0
            for cc in range(NCC):
                sl = SQSL[cc % 2]
                sq = Hs(sl)
                SC.wait(s_x, x_ld)
                SC.wait(s_t, sq_mm.get(sl, 0))
                SC.run(lambda e, o=sq, i=Af[:, cc, :]: e.activation(
                    out=o, in_=i, func=AF.Square).then_inc(s_a.h, 1))
                a_sq = s_a.inc()
                TE.wait(s_x, x_ld)
                TE.wait(s_v, last_stat_v)
                TE.run(lambda e, i=Af[:, cc, :], st=(cc == 0): e.matmul(
                    out=PSB[0].ap()[0:1, :], lhsT=ONES.ap(), rhs=i, start=st,
                    stop=(cc == NCC - 1)))
                TE.wait(s_a, a_sq)
                TE.run(lambda e, i2=sq, st=(cc == 0): e.matmul(
                    out=PSB[1].ap()[0:1, :], lhsT=ONES.ap(), rhs=i2, start=st,
                    stop=(cc == NCC - 1)).then_inc(s_t.h, 1))
                ln_mm = s_t.inc()
                sq_mm[sl] = ln_mm

            VE.wait(s_t, ln_mm)
            VE.wait(s_sp, mrs_spill)
            VE.run(lambda e: e.tensor_single_scalar(out=mrow, in_=PSB[0].ap()[0:1, :],
                                                    scalar=1.0 / 2048.0, op=OP.mult))
            VE.run(lambda e: e.tensor_single_scalar(out=rrow, in_=PSB[1].ap()[0:1, :],
                                                    scalar=1.0 / 2048.0, op=OP.mult))
            VE.run(lambda e: e.tensor_mul(PSB[0].ap()[0:1, :], mrow, mrow))
            VE.run(lambda e: e.tensor_sub(rrow, rrow, PSB[0].ap()[0:1, :]))
            VE.run(lambda e: e.tensor_single_scalar(out=rrow, in_=rrow, scalar=LN_EPS,
                                                    op=OP.add).then_inc(s_v.h, 1))
            last_stat_v = s_v.inc()
            SC.wait(s_v, last_stat_v)
            SC.run(lambda e: e.activation(out=rrow, in_=rrow, func=AF.Ln))
            SC.run(lambda e: e.activation(out=rrow, in_=rrow, func=AF.Exp,
                                          scale=-0.5).then_inc(s_a.h, 1))
            a_rstd = s_a.inc()
            if real:
                SY.wait(s_a, a_rstd)
                dma(SY, s_sp, mrs_sc[r, 0], mrow)
                mrs_spill = dma(SY, s_sp, mrs_sc[r, 1], rrow)
            TE.wait(s_a, a_rstd)
            TE.wait(s_v, last_norm_v)
            TE.wait(s_a, evac_a.get(2, 0))
            TE.wait(s_a, evac_a.get(3, 0))
            TE.run(lambda e: e.matmul(out=PSB[2].ap(), lhsT=ONESR.ap(), rhs=mrow,
                                      start=True, stop=True))
            TE.run(lambda e: e.matmul(out=PSB[3].ap(), lhsT=ONESR.ap(), rhs=rrow,
                                      start=True, stop=True).then_inc(s_t.h, 1))
            t_bc = s_t.inc()

            VE.wait(s_t, t_bc)
            for cc in range(NCC):
                VE.run(lambda e, cc=cc: e.tensor_sub(Bf[:, cc, :], Af[:, cc, :], PSB[2].ap()))
                VE.run(lambda e, cc=cc: e.tensor_mul(Bf[:, cc, :], Bf[:, cc, :], PSB[3].ap()))
                if real:
                    sl = cc % 2
                    VE.wait(s_sp, eslot_sp.get(sl, 0))
                    VE.run(lambda e, cc=cc, sl=sl: e.tensor_copy(
                        out=Es(sl), in_=Bf[:, cc, :]).then_inc(s_v.h, 1))
                    v_xnb = s_v.inc()
                    SY.wait(s_v, v_xnb)
                    eslot_sp[sl] = dma(SY, s_sp, xnb_sc[r, cc], Es(sl))
            VE.run(lambda e: e.engine_nop().then_inc(s_v.h, 1))
            last_norm_v = s_v.inc()

            ln_mm2 = 0
            for cc in range(NCC):
                sl = SQSL[cc % 2]
                sq = Hs(sl)
                SC.wait(s_v, last_norm_v)
                SC.wait(s_t, sq_mm.get(sl, 0))
                SC.run(lambda e, o=sq, i=Bf[:, cc, :]: e.activation(
                    out=o, in_=i, func=AF.Square).then_inc(s_a.h, 1))
                a_sq = s_a.inc()
                TE.wait(s_v, last_stat_v)
                TE.wait(s_v, last_norm_v)
                TE.run(lambda e, i=Bf[:, cc, :], st=(cc == 0): e.matmul(
                    out=PSB[0].ap()[0:1, :], lhsT=ONES.ap(), rhs=i, start=st,
                    stop=(cc == NCC - 1)))
                TE.wait(s_a, a_sq)
                TE.run(lambda e, i2=sq, st=(cc == 0): e.matmul(
                    out=PSB[1].ap()[0:1, :], lhsT=ONES.ap(), rhs=i2, start=st,
                    stop=(cc == NCC - 1)).then_inc(s_t.h, 1))
                ln_mm2 = s_t.inc()
                sq_mm[sl] = ln_mm2

            VE.wait(s_t, ln_mm2)
            VE.wait(s_sp, mrs_spill)
            VE.run(lambda e: e.tensor_single_scalar(out=mrow, in_=PSB[0].ap()[0:1, :],
                                                    scalar=1.0 / 2048.0, op=OP.mult))
            VE.run(lambda e: e.tensor_single_scalar(out=rrow, in_=PSB[1].ap()[0:1, :],
                                                    scalar=1.0 / 2048.0, op=OP.mult))
            VE.run(lambda e: e.tensor_mul(PSB[0].ap()[0:1, :], mrow, mrow))
            VE.run(lambda e: e.tensor_sub(rrow, rrow, PSB[0].ap()[0:1, :]))
            VE.run(lambda e: e.tensor_single_scalar(out=rrow, in_=rrow, scalar=LN_EPS,
                                                    op=OP.add).then_inc(s_v.h, 1))
            last_stat_v = s_v.inc()
            SC.wait(s_v, last_stat_v)
            SC.run(lambda e: e.activation(out=rrow, in_=rrow, func=AF.Ln))
            SC.run(lambda e: e.activation(out=rrow, in_=rrow, func=AF.Exp,
                                          scale=-0.5).then_inc(s_a.h, 1))
            a_rstd2 = s_a.inc()
            TE.wait(s_a, a_rstd2)
            TE.wait(s_v, v_mixes)
            TE.wait(s_a, evac_a.get(2, 0))
            TE.wait(s_a, evac_a.get(3, 0))
            TE.run(lambda e: e.matmul(out=PSB[2].ap(), lhsT=ONESR.ap(), rhs=mrow,
                                      start=True, stop=True))
            TE.run(lambda e: e.matmul(out=PSB[3].ap(), lhsT=ONESR.ap(), rhs=rrow,
                                      start=True, stop=True).then_inc(s_t.h, 1))
            t_bc2 = s_t.inc()

            xn2 = Hs(XN2s); prev = Hs(PREVs); tmp = Hs(TMPs)
            VE.wait(s_t, t_bc2)
            nq = 3 if real else 2
            for cc in range(NCC):
                VE.run(lambda e, cc=cc, o=xn2: e.tensor_sub(o, Bf[:, cc, :], PSB[2].ap()))
                VE.run(lambda e, o=xn2: e.tensor_mul(o, o, PSB[3].ap()))
                VE.run(lambda e, o=xn2, cc=cc: e.tensor_scalar(
                    out=o, in0=o, scalar1=vslice(0, cc), scalar2=vslice(1, cc),
                    op0=OP.mult, op1=OP.add))
                VE.run(lambda e, o=prev, i=xn2: e.tensor_copy(out=o[:, 1:TT], in_=i[:, 0:TT - 1]))
                if tau == 1:
                    VE.run(lambda e, o=prev, cc=cc: e.tensor_single_scalar(
                        out=o[:, 0:1], in_=CX.ap()[:, cc:cc + 1], scalar=PM.ap(), op=OP.mult))
                else:
                    VE.run(lambda e, o=prev, cc=cc: e.tensor_copy(
                        out=o[:, 0:1], in_=CX.ap()[:, cc:cc + 1]))
                VE.run(lambda e, i=xn2, cc=cc: e.tensor_copy(
                    out=CX.ap()[:, cc:cc + 1], in_=i[:, TT - 1:TT]))
                VE.run(lambda e, o=tmp, i=xn2, pr=prev: e.tensor_sub(o, i, pr))
                for q in range(nq):
                    VE.run(lambda e, q=q, cc=cc, t=tmp, pr=prev: e.scalar_tensor_tensor(
                        out=Cm[:, q, cc, :], in0=t, scalar=vslice(2 + q, cc), in1=pr,
                        op0=OP.mult, op1=OP.add))
            VE.run(lambda e: e.engine_nop().then_inc(s_v.h, 1))
            v_mixes = s_v.inc()

            for oc in range(NCC):
                banks = (4, 5, 6) if oc % 2 == 0 else (7, 2, 3)
                strip_ld = []
                for q in range(nq):
                    slot = q * 2 + oc % 2
                    SY.wait(s_g, cast_ready[(q, oc)])
                    SY.wait(s_t, kv_mm_done.get(slot, 0))
                    ld = dma(SY, s_x, Gs(slot), wq_sc[q, oc])
                    strip_ld.append(ld)
                if oc < 2:
                    TE.wait(s_v, v_mixes)
                mmq = []
                for q in range(nq):
                    slot = q * 2 + oc % 2
                    bank = banks[q]
                    lt = [Gs(slot)[:, ic * P:(ic + 1) * P] for ic in range(NCC)]
                    rh = [Cm[:, q, ic, :] for ic in range(NCC)]
                    t_mm = mm_acc(bank, lt, rh, waits=(
                        (s_x, strip_ld[q]), (s_v, v_mixes),
                        (s_a, evac_a.get(bank, 0)),
                        (s_v, psum_v.get(bank, 0))))
                    mmq.append(t_mm)
                    kv_mm_done[slot] = t_mm

                kf = Hs(KF[oc % 2]); vf = Hs(VF[oc % 2]); rf = Hs(RF[oc % 2])
                SC.wait(s_t, mmq[0])
                SC.wait(s_v, scan_v.get(oc % 2, 0))
                SC.run(lambda e, o=kf, b=banks[0]: e.activation(
                    out=o, in_=PSB[b].ap(), func=AF.Copy).then_inc(s_a.h, 1))
                evac_a[banks[0]] = s_a.inc()
                SC.wait(s_t, mmq[1])
                SC.run(lambda e, o=vf, b=banks[1]: e.activation(
                    out=o, in_=PSB[b].ap(), func=AF.Copy).then_inc(s_a.h, 1))
                evac_a[banks[1]] = s_a.inc()
                if real:
                    SC.wait(s_t, mmq[2])
                    SC.run(lambda e, o=rf, b=banks[2]: e.activation(
                        out=o, in_=PSB[b].ap(), func=AF.Sigmoid).then_inc(s_a.h, 1))
                    evac_a[banks[2]] = s_a.inc()
                SC.wait(s_v, scan_v_prev)
                SC.run(lambda e, o=Hs(EKs), i=kf: e.activation(
                    out=o, in_=i, func=AF.Exp).then_inc(s_a.h, 1))
                a_ek = s_a.inc()

                ek = Hs(EKs); ekv = Hs(EKVs); t1 = Hs(T1s); t2 = Hs(T2s)
                num = Hs(NUMs); den = Hs(DENs); rec = Hs(RECs)
                VE.wait(s_a, a_ek)
                VE.run(lambda e, cc=oc: e.tensor_single_scalar(
                    out=t2, in_=ek, scalar=EUc[:, cc:cc + 1], op=OP.mult))
                VE.run(lambda e, vf=vf: e.tensor_mul(t1, t2, vf))
                VE.run(lambda e, vf=vf: e.tensor_mul(ekv, ek, vf))
                if tau == 1:
                    VE.run(lambda e, cc=oc: e.tensor_single_scalar(
                        out=ABu.ap()[:, 0:1], in_=CA.ap()[:, cc:cc + 1], scalar=PM.ap(),
                        op=OP.mult))
                    VE.run(lambda e, cc=oc: e.tensor_single_scalar(
                        out=BBu.ap()[:, 0:1], in_=CB.ap()[:, cc:cc + 1], scalar=PM.ap(),
                        op=OP.mult))
                    VE.run(lambda e, cc=oc: e.scalar_tensor_tensor(
                        out=ABu.ap()[:, 1:2], in0=ABu.ap()[:, 0:1], scalar=EWc[:, cc:cc + 1],
                        in1=ekv[:, 0:1], op0=OP.mult, op1=OP.add))
                    VE.run(lambda e, cc=oc: e.scalar_tensor_tensor(
                        out=BBu.ap()[:, 1:2], in0=BBu.ap()[:, 0:1], scalar=EWc[:, cc:cc + 1],
                        in1=ek[:, 0:1], op0=OP.mult, op1=OP.add))
                    VE.run(lambda e: e.engine_nop())
                    VE.run(lambda e: e.engine_nop())
                    VE.run(lambda e, cc=oc: e.tensor_tensor_scan(
                        out=ABu.ap()[:, 2:TT + 1],
                        data0=EWc[:, cc:cc + 1].broadcast_to([P, TT - 1]),
                        data1=ekv[:, 1:TT], initial=ABu.ap()[:, 1:2],
                        op0=OP.mult, op1=OP.add))
                    VE.run(lambda e, cc=oc: e.tensor_tensor_scan(
                        out=BBu.ap()[:, 2:TT + 1],
                        data0=EWc[:, cc:cc + 1].broadcast_to([P, TT - 1]),
                        data1=ek[:, 1:TT], initial=BBu.ap()[:, 1:2],
                        op0=OP.mult, op1=OP.add))
                else:
                    VE.run(lambda e, cc=oc: e.tensor_copy(out=ABu.ap()[:, 0:1],
                                                          in_=CA.ap()[:, cc:cc + 1]))
                    VE.run(lambda e, cc=oc: e.tensor_copy(out=BBu.ap()[:, 0:1],
                                                          in_=CB.ap()[:, cc:cc + 1]))
                    VE.run(lambda e: e.engine_nop())
                    VE.run(lambda e: e.engine_nop())
                    VE.run(lambda e, cc=oc: e.tensor_tensor_scan(
                        out=ABu.ap()[:, 1:TT + 1],
                        data0=EWc[:, cc:cc + 1].broadcast_to([P, TT]),
                        data1=ekv, initial=ABu.ap()[:, 0:1], op0=OP.mult, op1=OP.add))
                    VE.run(lambda e, cc=oc: e.tensor_tensor_scan(
                        out=BBu.ap()[:, 1:TT + 1],
                        data0=EWc[:, cc:cc + 1].broadcast_to([P, TT]),
                        data1=ek, initial=BBu.ap()[:, 0:1], op0=OP.mult, op1=OP.add))
                VE.run(lambda e, cc=oc: e.tensor_copy(out=CA.ap()[:, cc:cc + 1],
                                                      in_=ABu.ap()[:, TT:TT + 1]))
                VE.run(lambda e, cc=oc: e.tensor_copy(out=CB.ap()[:, cc:cc + 1],
                                                      in_=BBu.ap()[:, TT:TT + 1]))
                if real:
                    VE.run(lambda e: e.tensor_add(num, ABu.ap()[:, 0:TT], t1))
                    VE.run(lambda e: e.tensor_add(den, BBu.ap()[:, 0:TT], t2))
                    VE.run(lambda e: e.reciprocal(rec, den))
                    VE.run(lambda e: e.tensor_mul(num, num, rec))
                    esl = 2 + oc % 2
                    VE.wait(s_sp, eslot_sp.get(esl, 0))
                    VE.run(lambda e, esl=esl, rf=rf: e.tensor_mul(Es(esl), rf, num))
                VE.run(lambda e: e.engine_nop().then_inc(s_v.h, 1))
                scan_v[oc % 2] = scan_v_prev = s_v.inc()
                if real:
                    SY.wait(s_v, scan_v[oc % 2])
                    eslot_sp[2 + oc % 2] = dma(SY, s_sp, rwkv_sc[r, oc], Es(2 + oc % 2))

        barrier(SY, GP, TE, SC, VE)
        rw_ld = 0
        for cc in range(NCC):
            for r4 in range(4):
                base = (cc % 8) * 2048 + r4 * TT
                dst = (Abf if cc < 8 else Bbf)[:, base:base + TT]
                rw_ld = dma(SY, s_x, dst, rwkv_sc[r4, cc])

        def rwkv_view(kc, r4):
            base = (kc % 8) * 2048 + r4 * TT
            return (Abf if kc < 8 else Bbf)[:, base:base + TT]

        wo_slot_t = {}
        for mc in range(NCC):
            slot = mc % 2
            GP.wait(s_t, wo_slot_t.get(slot, 0))
            wo_ld = dma(GP, s_g, Gs(slot), wo_h[mc].rearrange("p c q -> p (c q)"))
            for r4 in range(4):
                par = (mc * 4 + r4) % 2
                bank = 4 + par
                lt = [Gs(slot)[:, kc * P:(kc + 1) * P] for kc in range(NCC)]
                rh = [rwkv_view(kc, r4) for kc in range(NCC)]
                t_mm = mm_acc(bank, lt, rh, waits=(
                    (s_g, wo_ld), (s_x, rw_ld), (s_a, evac_a.get(bank, 0))))
                wo_slot_t[slot] = t_mm
                esl = 4 + par
                SC.wait(s_t, t_mm)
                SC.wait(s_sp, eslot_sp.get(esl, 0))
                SC.run(lambda e, esl=esl, bank=bank: e.activation(
                    out=Es(esl), in_=PSB[bank].ap(), func=AF.Copy).then_inc(s_a.h, 1))
                a_ev = s_a.inc(); evac_a[bank] = a_ev
                SY.wait(s_a, a_ev)
                eslot_sp[esl] = dma(SY, s_sp, ao_sc[r4, mc], Es(esl))

        barrier(GP, TE, SC, SY)
        GP.wait(s_g, n_h)
        st1_slot_t = {}
        for b in range(NH):
            slot = b % 2
            GP.wait(s_t, st1_slot_t.get(slot, 0))
            w2_ld = dma(GP, s_g, Gs(slot), w2_h[b].rearrange("p c q -> p (c q)"))
            for dg in range(4):
                par = (b * 4 + dg) % 2
                bank = 6 + par
                TE.wait(s_g, w2_ld)
                TE.wait(s_a, evac_a.get(bank, 0))
                TE.run(lambda e, slot=slot, dg=dg, bank=bank: e.matmul(
                    out=PSB[bank].ap(), lhsT=H128B.ap(),
                    rhs=Gs(slot)[:, dg * TT:(dg + 1) * TT],
                    start=True, stop=True).then_inc(s_t.h, 1))
                t_mm = s_t.inc(); st1_slot_t[slot] = t_mm
                esl = 6 + par
                SC.wait(s_t, t_mm)
                SC.wait(s_sp, eslot_sp.get(esl, 0))
                SC.run(lambda e, esl=esl, bank=bank: e.activation(
                    out=Es(esl), in_=PSB[bank].ap(), func=AF.Copy).then_inc(s_a.h, 1))
                a_ev = s_a.inc(); evac_a[bank] = a_ev
                SY.wait(s_a, a_ev)
                eslot_sp[esl] = dma(SY, s_sp, w2a_sc[b, :, dg * TT:(dg + 1) * TT], Es(esl))
        sp_w2a = s_sp.n

        w2a_flat = w2a_sc.rearrange("b p f -> b (p f)")
        w2p_flat = w2p_sc.rearrange("b p f -> b (p f)")
        st2_slot_t = {}
        for blk in range(P):
            slot = blk % 2
            SY.wait(s_sp, sp_w2a)
            SY.wait(s_t, st2_slot_t.get(slot, 0))
            r_ld = dma(SY, s_x, Gs(slot)[0:64, :], w2a_flat[:, blk * 2048:(blk + 1) * 2048])
            for sg in range(4):
                par = (blk * 4 + sg) % 2
                bank = 6 + par
                TE.wait(s_x, r_ld)
                TE.wait(s_a, evac_a.get(bank, 0))
                TE.run(lambda e, slot=slot, sg=sg, bank=bank: e.matmul(
                    out=PSB[bank].ap()[0:64, :], lhsT=H64B.ap(),
                    rhs=Gs(slot)[0:64, sg * TT:(sg + 1) * TT],
                    start=True, stop=True).then_inc(s_t.h, 1))
                t_mm = s_t.inc(); st2_slot_t[slot] = t_mm
                esl = 6 + par
                SC.wait(s_t, t_mm)
                SC.wait(s_sp, eslot_sp.get(esl, 0))
                SC.run(lambda e, esl=esl, bank=bank: e.activation(
                    out=Es(esl)[0:64, :], in_=PSB[bank].ap()[0:64, :], func=AF.Copy,
                    scale=FWHT_SCALE).then_inc(s_a.h, 1))
                a_ev = s_a.inc(); evac_a[bank] = a_ev
                SY.wait(s_a, a_ev)
                o_ap = w2p_flat[:, blk * 2048 + sg * TT: blk * 2048 + (sg + 1) * TT]
                eslot_sp[esl] = dma(SY, s_sp, o_ap, Es(esl)[0:64, :])
        sp_w2p = s_sp.n

        barrier(SY, GP, TE, SC, VE)
        xnb_ld = 0
        for cc in range(NCC):
            for r4 in range(4):
                base = (cc % 8) * 2048 + r4 * TT
                dst = (Abf if cc < 8 else Bbf)[:, base:base + TT]
                xnb_ld = dma(SY, s_x, dst, xnb_sc[r4, cc])

        def xnb_view(dc, r4):
            base = (dc % 8) * 2048 + r4 * TT
            return (Abf if dc < 8 else Bbf)[:, base:base + TT]

        ffa_slot_t = {}
        hidmul_v = {}
        for hc in range(NH):
            s1 = hc % 2; s3 = 2 + hc % 2
            GP.wait(s_t, ffa_slot_t.get(s1, 0))
            GP.wait(s_t, ffa_slot_t.get(s3, 0))
            w1_ld = dma(GP, s_g, Gs(s1), w1_h[hc].rearrange("p c q -> p (c q)"))
            w3_ld = dma(GP, s_g, Gs(s3), w3_h[hc].rearrange("p c q -> p (c q)"))
            for r4 in range(4):
                par = r4 % 2
                bku = 4 + par * 2
                bkl = 5 + par * 2
                lt1 = [Gs(s1)[:, dc * P:(dc + 1) * P] for dc in range(NCC)]
                lt3 = [Gs(s3)[:, dc * P:(dc + 1) * P] for dc in range(NCC)]
                rh = [xnb_view(dc, r4) for dc in range(NCC)]
                t_ug = mm_acc(bku, lt1, rh, waits=(
                    (s_g, w1_ld), (s_x, xnb_ld), (s_a, evac_a.get(bku, 0))))
                ffa_slot_t[s1] = t_ug
                t_ul = mm_acc(bkl, lt3, rh, waits=(
                    (s_g, w3_ld), (s_v, hidmul_v.get(bkl, 0))))
                ffa_slot_t[s3] = t_ul
                hslot = Hs(SQSL[par])
                SC.wait(s_t, t_ug)
                SC.wait(s_v, hidmul_v.get(("h", par), 0))
                SC.run(lambda e, o=hslot, bku=bku: e.activation(
                    out=o, in_=PSB[bku].ap(), func=AF.Silu).then_inc(s_a.h, 1))
                a_silu = s_a.inc(); evac_a[bku] = a_silu
                esl = 6 + par
                VE.wait(s_a, a_silu)
                VE.wait(s_t, t_ul)
                VE.wait(s_sp, eslot_sp.get(esl, 0))
                VE.run(lambda e, esl=esl, o=hslot, bkl=bkl: e.tensor_mul(
                    Es(esl), o, PSB[bkl].ap()).then_inc(s_v.h, 1))
                v_hid = s_v.inc()
                hidmul_v[bkl] = v_hid
                hidmul_v[("h", par)] = v_hid
                SY.wait(s_v, v_hid)
                dma(SY, s_sp, hid_sc[hc, 2 * r4], Es(esl)[:, 0:FBT])
                eslot_sp[esl] = dma(SY, s_sp, hid_sc[hc, 2 * r4 + 1], Es(esl)[:, FBT:2 * FBT])
        sp_hid = s_sp.n

        barrier(SY, GP, TE, SC, VE)

        def w2q_view(ldc):
            if ldc < 2:
                return G.ap()[:, ldc * 8192:(ldc + 1) * 8192]
            return Hbf[:, (ldc - 2) * 8192:(ldc - 1) * 8192]

        ep_last_v = 0
        gsl_v = {}
        for qg in range(4):
            SY.wait(s_sp, sp_w2p)
            SY.wait(s_t, s_t.n)
            SY.wait(s_a, s_a.n)
            SY.wait(s_v, ep_last_v)
            cache_ld = 0
            for ldc in range(4):
                dc = qg * 4 + ldc
                for bg in range(8):
                    in_ap = (w2p_sc[bg * 8:(bg + 1) * 8, :, dc * P:(dc + 1) * P]
                             .rearrange("b p q -> p b q"))
                    dst = (w2q_view(ldc)[:, bg * 8 * P:(bg + 1) * 8 * P]
                           .rearrange("p (b q) -> p b q", b=8))
                    cache_ld = dma(SY, s_x, dst, in_ap)
            for tb in range(8):
                r4 = tb // 2
                half = tb % 2
                SY.wait(s_v, ep_last_v)
                SY.wait(s_t, s_t.n if (qg + tb) else 0)
                SY.wait(s_sp, sp_hid)
                hid_ld = 0
                for hcg in range(8):
                    in_ap = hid_sc[hcg * 8:(hcg + 1) * 8, tb].rearrange("h p t -> p h t")
                    dst = (C.ap()[:, hcg * 8 * FBT:(hcg + 1) * 8 * FBT]
                           .rearrange("p (h t) -> p h t", h=8))
                    hid_ld = dma(SY, s_x, dst, in_ap)
                x_in = xT_h.rearrange("(c p) t -> p c t", p=P)[:, :, HALO + tb * FBT:
                                                               HALO + (tb + 1) * FBT]
                x_ld = dma(SY, s_x, Af[:, :, 0:FBT], x_in)
                dma(SY, s_x, SR.ap()[:, 0:FBT], mrs_sc[r4, 0, half * FBT:(half + 1) * FBT])
                m_ld = dma(SY, s_x, SR.ap()[:, TT:TT + FBT],
                           mrs_sc[r4, 1, half * FBT:(half + 1) * FBT])
                ao_ld = 0
                for ldc in range(4):
                    dc = qg * 4 + ldc
                    ao_in = ao_sc[r4, dc][:, half * FBT:(half + 1) * FBT]
                    ao_ld = dma(SY, s_x, E.ap()[:, ldc * FBT:(ldc + 1) * FBT], ao_in)
                TE.wait(s_x, m_ld)
                TE.wait(s_v, ep_last_v)
                TE.run(lambda e: e.matmul(out=PSB[4].ap()[:, 0:FBT], lhsT=ONESR.ap(),
                                          rhs=SR.ap()[:, 0:FBT], start=True, stop=True))
                TE.run(lambda e: e.matmul(out=PSB[5].ap()[:, 0:FBT], lhsT=ONESR.ap(),
                                          rhs=SR.ap()[:, TT:TT + FBT], start=True,
                                          stop=True).then_inc(s_t.h, 1))
                t_bcst = s_t.inc()
                TE.wait(s_x, cache_ld)
                TE.wait(s_x, hid_ld)
                for sem, val in ((s_a, max(evac_a.get(i, 0) for i in range(4))),
                                 (s_v, ep_last_v)):
                    TE.wait(sem, val)
                for bp in range(NH):
                    for ldc in range(4):
                        last = bp == NH - 1 and ldc == 3
                        if last:
                            TE.run(lambda e, bp=bp, ldc=ldc: e.matmul(
                                out=PSB[ldc].ap()[:, 0:FBT],
                                lhsT=w2q_view(ldc)[:, bp * P:(bp + 1) * P],
                                rhs=Cht[:, bp, :], start=False,
                                stop=True).then_inc(s_t.h, 1))
                        else:
                            TE.run(lambda e, bp=bp, ldc=ldc, st=(bp == 0): e.matmul(
                                out=PSB[ldc].ap()[:, 0:FBT],
                                lhsT=w2q_view(ldc)[:, bp * P:(bp + 1) * P],
                                rhs=Cht[:, bp, :], start=st, stop=(bp == NH - 1)))
                t_acc = s_t.inc()
                VE.wait(s_x, x_ld)
                VE.wait(s_x, ao_ld)
                VE.wait(s_t, t_bcst)
                for ldc in range(4):
                    dc = qg * 4 + ldc
                    par = ldc % 2
                    gslt = S.ap()[:, par * FBT:(par + 1) * FBT]
                    xns = S.ap()[:, TT + par * FBT:TT + (par + 1) * FBT]
                    SC.wait(s_t, t_acc)
                    SC.wait(s_v, gsl_v.get(par, 0))
                    SC.run(lambda e, o=gslt, ldc=ldc: e.activation(
                        out=o, in_=PSB[ldc].ap()[:, 0:FBT], func=AF.Silu).then_inc(s_a.h, 1))
                    a_g = s_a.inc()
                    evac_a[ldc] = a_g
                    VE.run(lambda e, o=xns, dc=dc: e.tensor_sub(
                        o, Af[:, dc, 0:FBT], PSB[4].ap()[:, 0:FBT]))
                    VE.run(lambda e, o=xns: e.tensor_mul(o, o, PSB[5].ap()[:, 0:FBT]))
                    VE.run(lambda e, o=xns, ldc=ldc: e.tensor_add(
                        o, o, E.ap()[:, ldc * FBT:(ldc + 1) * FBT]))
                    VE.wait(s_a, a_g)
                    VE.run(lambda e, o=xns, g=gslt: e.tensor_mul(o, o, g))
                    VE.run(lambda e, o=Bf[:, dc, 0:FBT], dc=dc, xns=xns: e.tensor_add(
                        o, Af[:, dc, 0:FBT], xns).then_inc(s_v.h, 1))
                    v_out = s_v.inc()
                    gsl_v[par] = v_out
                    ep_last_v = v_out
                    SY.wait(s_v, v_out)
                    o_ap = out_h.rearrange("(c p) t -> p c t", p=P)[:, dc,
                                                                    tb * FBT:(tb + 1) * FBT]
                    dma(SY, s_sp, o_ap, Bf[:, dc, 0:FBT])

        SY.run(lambda e, v=s_sp.n: e.wait_ge(s_sp.h, v))
        SY.run(lambda e, v=s_x.n: e.wait_ge(s_x.h, v))
        GP.run(lambda e, v=s_g.n: e.wait_ge(s_g.h, v))

        @block.sync
        def _(e):
            for fn in SY.steps:
                fn(e)

        @block.gpsimd
        def _(e):
            for fn in GP.steps:
                fn(e)

        @block.scalar
        def _(e):
            for fn in SC.steps:
                fn(e)

        @block.vector
        def _(e):
            for fn in VE.steps:
                fn(e)

        @block.tensor
        def _(e):
            for fn in TE.steps:
                fn(e)

    return nc


def _hadamard(n):
    h = np.array([[1.0]], np.float32)
    while h.shape[0] < n:
        h = np.block([[h, h], [h, -h]]).astype(np.float32)
    return h


_NC_CACHE = {}


def kernel(L0_attn_x, L0_attn_ln_scale, L0_attn_ln_bias, L0_attn_mix_k,
           L0_attn_mix_v, L0_attn_mix_r, L0_attn_time_first, L0_attn_time_decay,
           L0_attn_Wk, L0_attn_Wv, L0_attn_Wr, L0_attn_Wo,
           L0_ffn_W1, L0_ffn_W2, L0_ffn_W3):
    X = np.asarray(L0_attn_x, np.float32)

    def pack_vec(v):
        return np.ascontiguousarray(np.asarray(v, np.float32).reshape(NCC, P).T)

    vecs = np.ascontiguousarray(np.stack(
        [pack_vec(L0_attn_ln_scale), pack_vec(L0_attn_ln_bias),
         pack_vec(L0_attn_mix_k), pack_vec(L0_attn_mix_v),
         pack_vec(L0_attn_mix_r), pack_vec(L0_attn_time_first),
         pack_vec(L0_attn_time_decay)], axis=1))

    def tile_sq(W):
        a = np.asarray(W, np.float32).reshape(NCC, P, NCC, P)
        return np.ascontiguousarray(a.transpose(2, 1, 0, 3))

    def tile_w13(W):
        a = np.asarray(W, np.float32).reshape(NH, P, NCC, P)
        return np.ascontiguousarray(a.transpose(0, 3, 2, 1))

    def tile_w2(W):
        a = np.asarray(W, np.float32).reshape(NCC, P, NH, P)
        return np.ascontiguousarray(a.transpose(2, 3, 0, 1))

    wk = tile_sq(L0_attn_Wk); wv = tile_sq(L0_attn_Wv)
    wr = tile_sq(L0_attn_Wr); wo = tile_sq(L0_attn_Wo)
    w1 = tile_w13(L0_ffn_W1); w3 = tile_w13(L0_ffn_W3)
    w2 = tile_w2(L0_ffn_W2)
    h128 = _hadamard(128)
    h64 = _hadamard(64)
    ones = np.ones((P, 1), np.float32)
    onesr = np.ones((1, P), np.float32)

    in_maps = []
    for c in range(8):
        b, pos = divmod(c, 4)
        t0 = pos * TREAL
        xT = np.zeros((2048, HALO + TREAL), np.float32)
        src = X[b, max(0, t0 - HALO):t0 + TREAL]
        xT[:, (HALO + TREAL) - src.shape[0]:] = src.T
        pm = np.full((P, 1), 0.0 if pos == 0 else 1.0, np.float32)
        in_maps.append(dict(
            xT=np.ascontiguousarray(xT), vecs=vecs, pmask=pm, ones=ones,
            onesr=onesr, h128=h128, h64=h64, wk=wk, wv=wv, wr=wr, wo=wo,
            w1=w1, w3=w3, w2=w2))

    if "nc" not in _NC_CACHE:
        _NC_CACHE["nc"] = build_program()
    nc = _NC_CACHE["nc"]
    res = run_bass_kernel_spmd(nc, in_maps, core_ids=list(range(8)))

    out = np.empty((2, 8192, 2048), np.float32)
    for c in range(8):
        b, pos = divmod(c, 4)
        out[b, pos * TREAL:(pos + 1) * TREAL] = np.asarray(res.results[c]["outT"]).T
    return out


# version 05
# speedup vs baseline: 1.3831x; 1.3831x over previous
import contextlib
import numpy as np
import concourse.bass as bass
import concourse.mybir as mybir
from concourse.bass_utils import run_bass_kernel_spmd

F32 = mybir.dt.float32
BF16 = mybir.dt.bfloat16
AF = mybir.ActivationFunctionType
OP = mybir.AluOpType

P = 128
NCC = 16
TT = 512
NT = 5
TREAL = 2048
HALO = 512
NH = 64
FBT = 256
LN_EPS = 1e-5
FWHT_SCALE = float(np.float32(1.0 / np.sqrt(np.float64(8192.0))))


class Sem:
    def __init__(self, handle, name):
        self.h = handle
        self.name = name
        self.n = 0

    def inc(self, k=1):
        self.n += k
        return self.n


class Eng:
    def __init__(self, name):
        self.name = name
        self.steps = []
        self.lw = {}

    def wait(self, sem, val):
        if val <= 0 or self.lw.get(sem.name, 0) >= val:
            return
        self.lw[sem.name] = val
        self.steps.append(lambda e, s=sem, v=val: e.wait_ge(s.h, v))

    def run(self, fn):
        self.steps.append(fn)


def build_program():
    nc = bass.Bass("TRN2", target_bir_lowering=False)

    xT_h = nc.dram_tensor("xT", [2048, 2560], F32, kind="ExternalInput").ap()
    vec_h = nc.dram_tensor("vecs", [P, 7, NCC], F32, kind="ExternalInput").ap()
    pm_h = nc.dram_tensor("pmask", [P, 1], F32, kind="ExternalInput").ap()
    ones_h = nc.dram_tensor("ones", [P, 1], F32, kind="ExternalInput").ap()
    onesr_h = nc.dram_tensor("onesr", [1, P], F32, kind="ExternalInput").ap()
    h128_h = nc.dram_tensor("h128", [P, P], F32, kind="ExternalInput").ap()
    h64_h = nc.dram_tensor("h64", [64, 64], F32, kind="ExternalInput").ap()
    wk_h = nc.dram_tensor("wk", [NCC, P, NCC, P], F32, kind="ExternalInput").ap()
    wv_h = nc.dram_tensor("wv", [NCC, P, NCC, P], F32, kind="ExternalInput").ap()
    wr_h = nc.dram_tensor("wr", [NCC, P, NCC, P], F32, kind="ExternalInput").ap()
    wo_h = nc.dram_tensor("wo", [NCC, P, NCC, P], F32, kind="ExternalInput").ap()
    w1_h = nc.dram_tensor("w1", [NH, P, NCC, P], F32, kind="ExternalInput").ap()
    w3_h = nc.dram_tensor("w3", [NH, P, NCC, P], F32, kind="ExternalInput").ap()
    w2_h = nc.dram_tensor("w2", [NH, P, NCC, P], F32, kind="ExternalInput").ap()
    out_h = nc.dram_tensor("outT", [2048, 2048], F32, kind="ExternalOutput").ap()

    wq_sc = nc.dram_tensor("wq_sc", [3, NCC, P, NCC * P], BF16, kind="Internal").ap()
    xnb_sc = nc.dram_tensor("xnb_sc", [4, NCC, P, TT], BF16, kind="Internal").ap()
    rwkv_sc = nc.dram_tensor("rwkv_sc", [4, NCC, P, TT], BF16, kind="Internal").ap()
    ao_sc = nc.dram_tensor("ao_sc", [4, NCC, P, TT], BF16, kind="Internal").ap()
    mrs_sc = nc.dram_tensor("mrs_sc", [4, 2, TT], F32, kind="Internal").ap()
    w2a_sc = nc.dram_tensor("w2a_sc", [NH, P, 2048], BF16, kind="Internal").ap()
    w2p_sc = nc.dram_tensor("w2p_sc", [NH, P, 2048], BF16, kind="Internal").ap()
    hid_sc = nc.dram_tensor("hid_sc", [NH, 8, P, FBT], BF16, kind="Internal").ap()

    with contextlib.ExitStack() as es:
        ec = es.enter_context
        A = ec(nc.sbuf_tensor("A", [P, 16 * TT], F32))
        B = ec(nc.sbuf_tensor("B", [P, 16 * TT], F32))
        C = ec(nc.sbuf_tensor("C", [P, 3 * 16 * TT], BF16))
        G = ec(nc.sbuf_tensor("G", [P, 16384], BF16))
        H = ec(nc.sbuf_tensor("H", [P, 18 * TT], F32))
        E = ec(nc.sbuf_tensor("E", [P, 4096], BF16))
        S = ec(nc.sbuf_tensor("S", [P, 1024], F32))
        ABu = ec(nc.sbuf_tensor("ABu", [P, TT + 1], F32))
        BBu = ec(nc.sbuf_tensor("BBu", [P, TT + 1], F32))
        SR = ec(nc.sbuf_tensor("SR", [1, 1024], F32))
        VEC = ec(nc.sbuf_tensor("VEC", [P, 7, NCC], F32))
        DER = ec(nc.sbuf_tensor("DER", [P, 64], F32))
        PM = ec(nc.sbuf_tensor("PMt", [P, 1], F32))
        ONES = ec(nc.sbuf_tensor("ONESt", [P, 1], F32))
        ONESR = ec(nc.sbuf_tensor("ONESRt", [1, P], F32))
        H128B = ec(nc.sbuf_tensor("H128Bt", [P, P], BF16))
        H64B = ec(nc.sbuf_tensor("H64Bt", [64, 64], BF16))
        CA = ec(nc.sbuf_tensor("CAt", [P, NCC], F32))
        CB = ec(nc.sbuf_tensor("CBt", [P, NCC], F32))
        CX = ec(nc.sbuf_tensor("CXt", [P, NCC], F32))
        PSB = [ec(nc.psum_tensor(f"PSB{i}", [P, TT], F32)) for i in range(8)]

        s_x = Sem(ec(nc.semaphore("s_x")), "s_x")
        s_g = Sem(ec(nc.semaphore("s_g")), "s_g")
        s_sp = Sem(ec(nc.semaphore("s_sp")), "s_sp")
        s_t = Sem(ec(nc.semaphore("s_t")), "s_t")
        s_v = Sem(ec(nc.semaphore("s_v")), "s_v")
        s_a = Sem(ec(nc.semaphore("s_a")), "s_a")
        block = ec(nc.Block())

        SY = Eng("sync"); GP = Eng("gpsimd"); SC = Eng("scalar")
        VE = Eng("vector"); TE = Eng("tensor")
        ALL_ENG = (SY, GP, SC, VE, TE)

        def barrier(*engs):
            for eng in engs:
                for sem in (s_t, s_v, s_a, s_x, s_g, s_sp):
                    eng.wait(sem, sem.n)

        Af = A.ap().rearrange("p (c t) -> p c t", c=16)
        Bf = B.ap().rearrange("p (c t) -> p c t", c=16)
        Abf = A.ap().bitcast(BF16)
        Bbf = B.ap().bitcast(BF16)
        Hbf = H.ap().bitcast(BF16)
        Cm = C.ap().rearrange("p (q c t) -> p q c t", q=3, c=16)
        Cht = C.ap().rearrange("p (h t) -> p h t", t=FBT)

        def Hs(i):
            return H.ap()[:, i * TT:(i + 1) * TT]

        def Es(i):
            return E.ap()[:, i * TT:(i + 1) * TT]

        def Gs(i):
            return G.ap()[:, i * 2048:(i + 1) * 2048]

        EWc = DER.ap()[:, 16:32]
        EUc = DER.ap()[:, 32:48]

        def vslice(j, cc):
            return VEC.ap()[:, j, cc:cc + 1]

        def dma(ENG, sem, out, in_):
            ENG.run(lambda e, o=out, i=in_, s=sem: e.dma_start(out=o, in_=i).then_inc(s.h, 16))
            return sem.inc(16)

        def mm_acc(bank, lhsT_list, rhs_list, waits=()):
            n = len(lhsT_list)
            for sem, val in waits:
                TE.wait(sem, val)
            for i in range(n):
                last = i == n - 1
                lt, rh = lhsT_list[i], rhs_list[i]
                if last:
                    TE.run(lambda e, lt=lt, rh=rh, b=bank, st=(i == 0): e.matmul(
                        out=PSB[b].ap(), lhsT=lt, rhs=rh, start=st,
                        stop=True).then_inc(s_t.h, 1))
                else:
                    TE.run(lambda e, lt=lt, rh=rh, b=bank, st=(i == 0): e.matmul(
                        out=PSB[b].ap(), lhsT=lt, rhs=rh, start=st, stop=False))
            return s_t.inc()

        evac_a = {}
        psum_v = {}
        sq_mm = {}
        gslot_t = {}
        gslot_v = {}
        eslot_sp = {}

        n_consts = 0
        for out, in_ in ((VEC.ap(), vec_h), (PM.ap(), pm_h), (ONES.ap(), ones_h),
                         (ONESR.ap(), onesr_h)):
            n_consts = dma(SY, s_x, out, in_)
        n_h = 0
        for out, in_ in ((H128B.ap(), h128_h), (H64B.ap(), h64_h)):
            n_h = dma(GP, s_g, out, in_)

        SC.wait(s_x, n_consts)
        SC.run(lambda e: e.activation(out=DER.ap()[:, 0:16], in_=VEC.ap()[:, 6, :],
                                      func=AF.Exp).then_inc(s_a.h, 1))
        a0 = s_a.inc()
        VE.wait(s_x, n_consts)
        VE.wait(s_a, a0)
        VE.run(lambda e: e.tensor_single_scalar(out=DER.ap()[:, 0:16], in_=DER.ap()[:, 0:16],
                                                scalar=-1.0, op=OP.mult).then_inc(s_v.h, 1))
        v0 = s_v.inc()
        SC.wait(s_v, v0)
        SC.run(lambda e: e.activation(out=DER.ap()[:, 16:32], in_=DER.ap()[:, 0:16], func=AF.Exp))
        SC.run(lambda e: e.activation(out=DER.ap()[:, 32:48], in_=VEC.ap()[:, 5, :],
                                      func=AF.Exp).then_inc(s_a.h, 1))
        a_der = s_a.inc()
        VE.run(lambda e: e.memset(CA.ap(), 0.0))
        VE.run(lambda e: e.memset(CB.ap(), 0.0))
        VE.run(lambda e: e.memset(CX.ap(), 0.0).then_inc(s_v.h, 1))
        s_v.inc()

        wqh = (wk_h, wv_h, wr_h)
        cast_slot_store = {6: 0, 7: 0}
        cast_ready = {}
        for q in range(3):
            for mc in range(NCC):
                slot = 6 + (q * NCC + mc) % 2
                GP.wait(s_g, cast_slot_store[slot])
                g_ld = dma(GP, s_g, Gs(slot), wqh[q][mc].rearrange("p c q -> p (c q)"))
                GP.wait(s_g, g_ld)
                cast_slot_store[slot] = dma(GP, s_g, wq_sc[q, mc], Gs(slot))
                cast_ready[(q, mc)] = cast_slot_store[slot]
        g_castpass = s_g.n

        SQSL = (0, 1)
        KF = (2, 3); VF = (4, 5); RF = (6, 7)
        EKs, EKVs, T1s, T2s, NUMs, DENs, RECs = 8, 9, 10, 11, 12, 13, 14
        XN2s, PREVs, TMPs = 15, 16, 17

        mrow = SR.ap()[:, 0:TT]; rrow = SR.ap()[:, TT:2 * TT]

        last_norm_v = 0
        last_stat_v = 0
        mrs_spill = 0
        v_mixes = 0
        kv_mm_done = {}
        scan_v = {}
        scan_v_prev = 0

        for tau in range(NT):
            real = tau >= 1
            r = tau - 1

            xre = xT_h.rearrange("(c p) t -> p c t", p=P)
            SY.wait(s_v, last_norm_v)
            x_ld = 0
            for ccg in range(4):
                x_ld = dma(SY, s_x, Af[:, ccg * 4:(ccg + 1) * 4, :],
                           xre[:, ccg * 4:(ccg + 1) * 4, TT * tau: TT * (tau + 1)])

            ln_mm = 0
            for cc in range(NCC):
                sl = SQSL[cc % 2]
                sq = Hs(sl)
                SC.wait(s_x, x_ld)
                SC.wait(s_t, sq_mm.get(sl, 0))
                SC.run(lambda e, o=sq, i=Af[:, cc, :]: e.activation(
                    out=o, in_=i, func=AF.Square).then_inc(s_a.h, 1))
                a_sq = s_a.inc()
                TE.wait(s_x, x_ld)
                TE.wait(s_v, last_stat_v)
                TE.run(lambda e, i=Af[:, cc, :], st=(cc == 0): e.matmul(
                    out=PSB[0].ap()[0:1, :], lhsT=ONES.ap(), rhs=i, start=st,
                    stop=(cc == NCC - 1)))
                TE.wait(s_a, a_sq)
                TE.run(lambda e, i2=sq, st=(cc == 0): e.matmul(
                    out=PSB[1].ap()[0:1, :], lhsT=ONES.ap(), rhs=i2, start=st,
                    stop=(cc == NCC - 1)).then_inc(s_t.h, 1))
                ln_mm = s_t.inc()
                sq_mm[sl] = ln_mm

            VE.wait(s_t, ln_mm)
            VE.wait(s_sp, mrs_spill)
            VE.run(lambda e: e.tensor_single_scalar(out=mrow, in_=PSB[0].ap()[0:1, :],
                                                    scalar=1.0 / 2048.0, op=OP.mult))
            VE.run(lambda e: e.tensor_single_scalar(out=rrow, in_=PSB[1].ap()[0:1, :],
                                                    scalar=1.0 / 2048.0, op=OP.mult))
            VE.run(lambda e: e.tensor_mul(PSB[0].ap()[0:1, :], mrow, mrow))
            VE.run(lambda e: e.tensor_sub(rrow, rrow, PSB[0].ap()[0:1, :]))
            VE.run(lambda e: e.tensor_single_scalar(out=rrow, in_=rrow, scalar=LN_EPS,
                                                    op=OP.add).then_inc(s_v.h, 1))
            last_stat_v = s_v.inc()
            SC.wait(s_v, last_stat_v)
            SC.run(lambda e: e.activation(out=rrow, in_=rrow, func=AF.Ln))
            SC.run(lambda e: e.activation(out=rrow, in_=rrow, func=AF.Exp,
                                          scale=-0.5).then_inc(s_a.h, 1))
            a_rstd = s_a.inc()
            if real:
                dma(SC, s_sp, mrs_sc[r, 0], mrow)
                mrs_spill = dma(SC, s_sp, mrs_sc[r, 1], rrow)
            TE.wait(s_a, a_rstd)
            TE.wait(s_v, last_norm_v)
            TE.wait(s_a, evac_a.get(2, 0))
            TE.wait(s_a, evac_a.get(3, 0))
            TE.run(lambda e: e.matmul(out=PSB[2].ap(), lhsT=ONESR.ap(), rhs=mrow,
                                      start=True, stop=True))
            TE.run(lambda e: e.matmul(out=PSB[3].ap(), lhsT=ONESR.ap(), rhs=rrow,
                                      start=True, stop=True).then_inc(s_t.h, 1))
            t_bc = s_t.inc()

            VE.wait(s_t, t_bc)
            for cc in range(NCC):
                VE.run(lambda e, cc=cc: e.tensor_sub(Bf[:, cc, :], Af[:, cc, :], PSB[2].ap()))
                VE.run(lambda e, cc=cc: e.tensor_mul(Bf[:, cc, :], Bf[:, cc, :], PSB[3].ap()))
                if real:
                    sl = cc % 2
                    VE.wait(s_sp, eslot_sp.get(sl, 0))
                    VE.run(lambda e, cc=cc, sl=sl: e.tensor_copy(
                        out=Es(sl), in_=Bf[:, cc, :]).then_inc(s_v.h, 1))
                    v_xnb = s_v.inc()
                    SC.wait(s_v, v_xnb)
                    eslot_sp[sl] = dma(SC, s_sp, xnb_sc[r, cc], Es(sl))
            VE.run(lambda e: e.engine_nop().then_inc(s_v.h, 1))
            last_norm_v = s_v.inc()

            ln_mm2 = 0
            for cc in range(NCC):
                sl = SQSL[cc % 2]
                sq = Hs(sl)
                SC.wait(s_v, last_norm_v)
                SC.wait(s_t, sq_mm.get(sl, 0))
                SC.run(lambda e, o=sq, i=Bf[:, cc, :]: e.activation(
                    out=o, in_=i, func=AF.Square).then_inc(s_a.h, 1))
                a_sq = s_a.inc()
                TE.wait(s_v, last_stat_v)
                TE.wait(s_v, last_norm_v)
                TE.run(lambda e, i=Bf[:, cc, :], st=(cc == 0): e.matmul(
                    out=PSB[0].ap()[0:1, :], lhsT=ONES.ap(), rhs=i, start=st,
                    stop=(cc == NCC - 1)))
                TE.wait(s_a, a_sq)
                TE.run(lambda e, i2=sq, st=(cc == 0): e.matmul(
                    out=PSB[1].ap()[0:1, :], lhsT=ONES.ap(), rhs=i2, start=st,
                    stop=(cc == NCC - 1)).then_inc(s_t.h, 1))
                ln_mm2 = s_t.inc()
                sq_mm[sl] = ln_mm2

            VE.wait(s_t, ln_mm2)
            VE.wait(s_sp, mrs_spill)
            VE.run(lambda e: e.tensor_single_scalar(out=mrow, in_=PSB[0].ap()[0:1, :],
                                                    scalar=1.0 / 2048.0, op=OP.mult))
            VE.run(lambda e: e.tensor_single_scalar(out=rrow, in_=PSB[1].ap()[0:1, :],
                                                    scalar=1.0 / 2048.0, op=OP.mult))
            VE.run(lambda e: e.tensor_mul(PSB[0].ap()[0:1, :], mrow, mrow))
            VE.run(lambda e: e.tensor_sub(rrow, rrow, PSB[0].ap()[0:1, :]))
            VE.run(lambda e: e.tensor_single_scalar(out=rrow, in_=rrow, scalar=LN_EPS,
                                                    op=OP.add).then_inc(s_v.h, 1))
            last_stat_v = s_v.inc()
            SC.wait(s_v, last_stat_v)
            SC.run(lambda e: e.activation(out=rrow, in_=rrow, func=AF.Ln))
            SC.run(lambda e: e.activation(out=rrow, in_=rrow, func=AF.Exp,
                                          scale=-0.5).then_inc(s_a.h, 1))
            a_rstd2 = s_a.inc()
            TE.wait(s_a, a_rstd2)
            TE.wait(s_v, v_mixes)
            TE.wait(s_a, evac_a.get(2, 0))
            TE.wait(s_a, evac_a.get(3, 0))
            TE.run(lambda e: e.matmul(out=PSB[2].ap(), lhsT=ONESR.ap(), rhs=mrow,
                                      start=True, stop=True))
            TE.run(lambda e: e.matmul(out=PSB[3].ap(), lhsT=ONESR.ap(), rhs=rrow,
                                      start=True, stop=True).then_inc(s_t.h, 1))
            t_bc2 = s_t.inc()

            xn2 = Hs(XN2s); prev = Hs(PREVs); tmp = Hs(TMPs)
            VE.wait(s_t, t_bc2)
            nq = 3 if real else 2
            for cc in range(NCC):
                VE.run(lambda e, cc=cc, o=xn2: e.tensor_sub(o, Bf[:, cc, :], PSB[2].ap()))
                VE.run(lambda e, o=xn2: e.tensor_mul(o, o, PSB[3].ap()))
                VE.run(lambda e, o=xn2, cc=cc: e.tensor_scalar(
                    out=o, in0=o, scalar1=vslice(0, cc), scalar2=vslice(1, cc),
                    op0=OP.mult, op1=OP.add))
                VE.run(lambda e, o=prev, i=xn2: e.tensor_copy(out=o[:, 1:TT], in_=i[:, 0:TT - 1]))
                if tau == 1:
                    VE.run(lambda e, o=prev, cc=cc: e.tensor_single_scalar(
                        out=o[:, 0:1], in_=CX.ap()[:, cc:cc + 1], scalar=PM.ap(), op=OP.mult))
                else:
                    VE.run(lambda e, o=prev, cc=cc: e.tensor_copy(
                        out=o[:, 0:1], in_=CX.ap()[:, cc:cc + 1]))
                VE.run(lambda e, i=xn2, cc=cc: e.tensor_copy(
                    out=CX.ap()[:, cc:cc + 1], in_=i[:, TT - 1:TT]))
                VE.run(lambda e, o=tmp, i=xn2, pr=prev: e.tensor_sub(o, i, pr))
                for q in range(nq):
                    VE.run(lambda e, q=q, cc=cc, t=tmp, pr=prev: e.scalar_tensor_tensor(
                        out=Cm[:, q, cc, :], in0=t, scalar=vslice(2 + q, cc), in1=pr,
                        op0=OP.mult, op1=OP.add))
            VE.run(lambda e: e.engine_nop().then_inc(s_v.h, 1))
            v_mixes = s_v.inc()

            def load_strips(oc2):
                lds = []
                for q in range(nq):
                    slot = q * 2 + oc2 % 2
                    SY.wait(s_g, cast_ready[(q, oc2)])
                    SY.wait(s_t, kv_mm_done.get(slot, 0))
                    lds.append(dma(SY, s_x, Gs(slot), wq_sc[q, oc2]))
                return lds
            strip_pending = {0: load_strips(0)}
            for oc in range(NCC):
                banks = (4, 5, 6) if oc % 2 == 0 else (7, 2, 3)
                strip_ld = strip_pending.pop(oc)
                if oc + 1 < NCC:
                    strip_pending[oc + 1] = load_strips(oc + 1)
                if oc < 2:
                    TE.wait(s_v, v_mixes)
                mmq = []
                for q in range(nq):
                    slot = q * 2 + oc % 2
                    bank = banks[q]
                    lt = [Gs(slot)[:, ic * P:(ic + 1) * P] for ic in range(NCC)]
                    rh = [Cm[:, q, ic, :] for ic in range(NCC)]
                    t_mm = mm_acc(bank, lt, rh, waits=(
                        (s_x, strip_ld[q]), (s_v, v_mixes),
                        (s_a, evac_a.get(bank, 0)),
                        (s_v, psum_v.get(bank, 0))))
                    mmq.append(t_mm)
                    kv_mm_done[slot] = t_mm

                kf = Hs(KF[oc % 2]); vf = Hs(VF[oc % 2]); rf = Hs(RF[oc % 2])
                SC.wait(s_t, mmq[0])
                SC.wait(s_v, scan_v.get(oc % 2, 0))
                SC.run(lambda e, o=kf, b=banks[0]: e.activation(
                    out=o, in_=PSB[b].ap(), func=AF.Copy).then_inc(s_a.h, 1))
                evac_a[banks[0]] = s_a.inc()
                SC.wait(s_t, mmq[1])
                SC.run(lambda e, o=vf, b=banks[1]: e.activation(
                    out=o, in_=PSB[b].ap(), func=AF.Copy).then_inc(s_a.h, 1))
                evac_a[banks[1]] = s_a.inc()
                if real:
                    SC.wait(s_t, mmq[2])
                    SC.run(lambda e, o=rf, b=banks[2]: e.activation(
                        out=o, in_=PSB[b].ap(), func=AF.Sigmoid).then_inc(s_a.h, 1))
                    evac_a[banks[2]] = s_a.inc()
                SC.wait(s_v, scan_v_prev)
                SC.run(lambda e, o=Hs(EKs), i=kf: e.activation(
                    out=o, in_=i, func=AF.Exp).then_inc(s_a.h, 1))
                a_ek = s_a.inc()

                ek = Hs(EKs); ekv = Hs(EKVs); t1 = Hs(T1s); t2 = Hs(T2s)
                num = Hs(NUMs); den = Hs(DENs); rec = Hs(RECs)
                VE.wait(s_a, a_ek)
                VE.run(lambda e, cc=oc: e.tensor_single_scalar(
                    out=t2, in_=ek, scalar=EUc[:, cc:cc + 1], op=OP.mult))
                VE.run(lambda e, vf=vf: e.tensor_mul(t1, t2, vf))
                VE.run(lambda e, vf=vf: e.tensor_mul(ekv, ek, vf))
                if tau == 1:
                    VE.run(lambda e, cc=oc: e.tensor_single_scalar(
                        out=ABu.ap()[:, 0:1], in_=CA.ap()[:, cc:cc + 1], scalar=PM.ap(),
                        op=OP.mult))
                    VE.run(lambda e, cc=oc: e.tensor_single_scalar(
                        out=BBu.ap()[:, 0:1], in_=CB.ap()[:, cc:cc + 1], scalar=PM.ap(),
                        op=OP.mult))
                    VE.run(lambda e, cc=oc: e.scalar_tensor_tensor(
                        out=ABu.ap()[:, 1:2], in0=ABu.ap()[:, 0:1], scalar=EWc[:, cc:cc + 1],
                        in1=ekv[:, 0:1], op0=OP.mult, op1=OP.add))
                    VE.run(lambda e, cc=oc: e.scalar_tensor_tensor(
                        out=BBu.ap()[:, 1:2], in0=BBu.ap()[:, 0:1], scalar=EWc[:, cc:cc + 1],
                        in1=ek[:, 0:1], op0=OP.mult, op1=OP.add))
                    VE.run(lambda e: e.engine_nop())
                    VE.run(lambda e: e.engine_nop())
                    VE.run(lambda e, cc=oc: e.tensor_tensor_scan(
                        out=ABu.ap()[:, 2:TT + 1],
                        data0=EWc[:, cc:cc + 1].broadcast_to([P, TT - 1]),
                        data1=ekv[:, 1:TT], initial=ABu.ap()[:, 1:2],
                        op0=OP.mult, op1=OP.add))
                    VE.run(lambda e, cc=oc: e.tensor_tensor_scan(
                        out=BBu.ap()[:, 2:TT + 1],
                        data0=EWc[:, cc:cc + 1].broadcast_to([P, TT - 1]),
                        data1=ek[:, 1:TT], initial=BBu.ap()[:, 1:2],
                        op0=OP.mult, op1=OP.add))
                else:
                    VE.run(lambda e, cc=oc: e.tensor_copy(out=ABu.ap()[:, 0:1],
                                                          in_=CA.ap()[:, cc:cc + 1]))
                    VE.run(lambda e, cc=oc: e.tensor_copy(out=BBu.ap()[:, 0:1],
                                                          in_=CB.ap()[:, cc:cc + 1]))
                    VE.run(lambda e: e.engine_nop())
                    VE.run(lambda e: e.engine_nop())
                    VE.run(lambda e, cc=oc: e.tensor_tensor_scan(
                        out=ABu.ap()[:, 1:TT + 1],
                        data0=EWc[:, cc:cc + 1].broadcast_to([P, TT]),
                        data1=ekv, initial=ABu.ap()[:, 0:1], op0=OP.mult, op1=OP.add))
                    VE.run(lambda e, cc=oc: e.tensor_tensor_scan(
                        out=BBu.ap()[:, 1:TT + 1],
                        data0=EWc[:, cc:cc + 1].broadcast_to([P, TT]),
                        data1=ek, initial=BBu.ap()[:, 0:1], op0=OP.mult, op1=OP.add))
                VE.run(lambda e, cc=oc: e.tensor_copy(out=CA.ap()[:, cc:cc + 1],
                                                      in_=ABu.ap()[:, TT:TT + 1]))
                VE.run(lambda e, cc=oc: e.tensor_copy(out=CB.ap()[:, cc:cc + 1],
                                                      in_=BBu.ap()[:, TT:TT + 1]))
                if real:
                    VE.run(lambda e: e.tensor_add(num, ABu.ap()[:, 0:TT], t1))
                    VE.run(lambda e: e.tensor_add(den, BBu.ap()[:, 0:TT], t2))
                    VE.run(lambda e: e.reciprocal(rec, den))
                    VE.run(lambda e: e.tensor_mul(num, num, rec))
                    esl = 2 + oc % 2
                    VE.wait(s_sp, eslot_sp.get(esl, 0))
                    VE.run(lambda e, esl=esl, rf=rf: e.tensor_mul(Es(esl), rf, num))
                VE.run(lambda e: e.engine_nop().then_inc(s_v.h, 1))
                scan_v[oc % 2] = scan_v_prev = s_v.inc()
                if real:
                    SC.wait(s_v, scan_v[oc % 2])
                    eslot_sp[2 + oc % 2] = dma(SC, s_sp, rwkv_sc[r, oc], Es(2 + oc % 2))

        barrier(SY, GP, TE, SC, VE)
        rw_ld = 0
        for cc in range(NCC):
            for r4 in range(4):
                base = (cc % 8) * 2048 + r4 * TT
                dst = (Abf if cc < 8 else Bbf)[:, base:base + TT]
                rw_ld = dma(SY, s_x, dst, rwkv_sc[r4, cc])

        def rwkv_view(kc, r4):
            base = (kc % 8) * 2048 + r4 * TT
            return (Abf if kc < 8 else Bbf)[:, base:base + TT]

        wo_slot_t = {}
        def load_wo(mc2):
            slot = mc2 % 2
            GP.wait(s_t, wo_slot_t.get(slot, 0))
            return dma(GP, s_g, Gs(slot), wo_h[mc2].rearrange("p c q -> p (c q)"))
        wo_pending = {0: load_wo(0)}
        for mc in range(NCC):
            slot = mc % 2
            wo_ld = wo_pending.pop(mc)
            if mc + 1 < NCC:
                wo_pending[mc + 1] = load_wo(mc + 1)
            for r4 in range(4):
                par = (mc * 4 + r4) % 2
                bank = 4 + par
                lt = [Gs(slot)[:, kc * P:(kc + 1) * P] for kc in range(NCC)]
                rh = [rwkv_view(kc, r4) for kc in range(NCC)]
                t_mm = mm_acc(bank, lt, rh, waits=(
                    (s_g, wo_ld), (s_x, rw_ld), (s_a, evac_a.get(bank, 0))))
                wo_slot_t[slot] = t_mm
                esl = 4 + par
                SC.wait(s_t, t_mm)
                SC.wait(s_sp, eslot_sp.get(esl, 0))
                SC.run(lambda e, esl=esl, bank=bank: e.activation(
                    out=Es(esl), in_=PSB[bank].ap(), func=AF.Copy).then_inc(s_a.h, 1))
                a_ev = s_a.inc(); evac_a[bank] = a_ev
                SY.wait(s_a, a_ev)
                eslot_sp[esl] = dma(SY, s_sp, ao_sc[r4, mc], Es(esl))

        barrier(GP, TE, SC, SY)
        GP.wait(s_g, n_h)
        st1_slot_t = {}
        def load_w2(b2):
            slot = b2 % 4
            GP.wait(s_t, st1_slot_t.get(slot, 0))
            return dma(GP, s_g, Gs(slot), w2_h[b2].rearrange("p c q -> p (c q)"))
        w2_pending = {0: load_w2(0), 1: load_w2(1)}
        for b in range(NH):
            slot = b % 4
            w2_ld = w2_pending.pop(b)
            if b + 2 < NH:
                w2_pending[b + 2] = load_w2(b + 2)
            for dg in range(4):
                par = (b * 4 + dg) % 4
                bank = 4 + par
                TE.wait(s_g, w2_ld)
                TE.wait(s_a, evac_a.get(bank, 0))
                TE.run(lambda e, slot=slot, dg=dg, bank=bank: e.matmul(
                    out=PSB[bank].ap(), lhsT=H128B.ap(),
                    rhs=Gs(slot)[:, dg * TT:(dg + 1) * TT],
                    start=True, stop=True).then_inc(s_t.h, 1))
                t_mm = s_t.inc(); st1_slot_t[slot] = t_mm
                esl = 4 + par
                SC.wait(s_t, t_mm)
                SC.wait(s_sp, eslot_sp.get(esl, 0))
                SC.run(lambda e, esl=esl, bank=bank: e.activation(
                    out=Es(esl), in_=PSB[bank].ap(), func=AF.Copy).then_inc(s_a.h, 1))
                a_ev = s_a.inc(); evac_a[bank] = a_ev
                SY.wait(s_a, a_ev)
                eslot_sp[esl] = dma(SY, s_sp, w2a_sc[b, :, dg * TT:(dg + 1) * TT], Es(esl))
        sp_w2a = s_sp.n

        w2a_flat = w2a_sc.rearrange("b p f -> b (p f)")
        w2p_flat = w2p_sc.rearrange("b p f -> b (p f)")
        st2_slot_t = {}
        def load_blk(b2):
            slot = b2 % 4
            SY.wait(s_sp, sp_w2a)
            SY.wait(s_t, st2_slot_t.get(slot, 0))
            return dma(SY, s_x, Gs(slot)[0:64, :], w2a_flat[:, b2 * 2048:(b2 + 1) * 2048])
        blk_pending = {0: load_blk(0), 1: load_blk(1)}
        for blk in range(P):
            slot = blk % 4
            r_ld = blk_pending.pop(blk)
            if blk + 2 < P:
                blk_pending[blk + 2] = load_blk(blk + 2)
            for sg in range(4):
                par = (blk * 4 + sg) % 4
                bank = 4 + par
                TE.wait(s_x, r_ld)
                TE.wait(s_a, evac_a.get(bank, 0))
                TE.run(lambda e, slot=slot, sg=sg, bank=bank: e.matmul(
                    out=PSB[bank].ap()[0:64, :], lhsT=H64B.ap(),
                    rhs=Gs(slot)[0:64, sg * TT:(sg + 1) * TT],
                    start=True, stop=True).then_inc(s_t.h, 1))
                t_mm = s_t.inc(); st2_slot_t[slot] = t_mm
                esl = 4 + par
                SC.wait(s_t, t_mm)
                SC.wait(s_sp, eslot_sp.get(esl, 0))
                SC.run(lambda e, esl=esl, bank=bank: e.activation(
                    out=Es(esl)[0:64, :], in_=PSB[bank].ap()[0:64, :], func=AF.Copy,
                    scale=FWHT_SCALE).then_inc(s_a.h, 1))
                a_ev = s_a.inc(); evac_a[bank] = a_ev
                SY.wait(s_a, a_ev)
                o_ap = w2p_flat[:, blk * 2048 + sg * TT: blk * 2048 + (sg + 1) * TT]
                eslot_sp[esl] = dma(SY, s_sp, o_ap, Es(esl)[0:64, :])
        sp_w2p = s_sp.n

        barrier(SY, GP, TE, SC, VE)
        xnb_ld = 0
        for cc in range(NCC):
            for r4 in range(4):
                base = (cc % 8) * 2048 + r4 * TT
                dst = (Abf if cc < 8 else Bbf)[:, base:base + TT]
                xnb_ld = dma(SY, s_x, dst, xnb_sc[r4, cc])

        def xnb_view(dc, r4):
            base = (dc % 8) * 2048 + r4 * TT
            return (Abf if dc < 8 else Bbf)[:, base:base + TT]

        ffa_slot_t = {}
        hidmul_v = {}
        def load_w13(hc2):
            s1 = hc2 % 2; s3 = 2 + hc2 % 2
            GP.wait(s_t, ffa_slot_t.get(s1, 0))
            GP.wait(s_t, ffa_slot_t.get(s3, 0))
            return (dma(GP, s_g, Gs(s1), w1_h[hc2].rearrange("p c q -> p (c q)")),
                    dma(GP, s_g, Gs(s3), w3_h[hc2].rearrange("p c q -> p (c q)")))
        ffa_pending = {0: load_w13(0)}
        for hc in range(NH):
            s1 = hc % 2; s3 = 2 + hc % 2
            w1_ld, w3_ld = ffa_pending.pop(hc)
            if hc + 1 < NH:
                ffa_pending[hc + 1] = load_w13(hc + 1)
            for r4 in range(4):
                par = r4 % 2
                bku = 4 + par * 2
                bkl = 5 + par * 2
                lt1 = [Gs(s1)[:, dc * P:(dc + 1) * P] for dc in range(NCC)]
                lt3 = [Gs(s3)[:, dc * P:(dc + 1) * P] for dc in range(NCC)]
                rh = [xnb_view(dc, r4) for dc in range(NCC)]
                t_ug = mm_acc(bku, lt1, rh, waits=(
                    (s_g, w1_ld), (s_x, xnb_ld), (s_a, evac_a.get(bku, 0))))
                ffa_slot_t[s1] = t_ug
                t_ul = mm_acc(bkl, lt3, rh, waits=(
                    (s_g, w3_ld), (s_v, hidmul_v.get(bkl, 0))))
                ffa_slot_t[s3] = t_ul
                hslot = Hs(SQSL[par])
                SC.wait(s_t, t_ug)
                SC.wait(s_v, hidmul_v.get(("h", par), 0))
                SC.run(lambda e, o=hslot, bku=bku: e.activation(
                    out=o, in_=PSB[bku].ap(), func=AF.Silu).then_inc(s_a.h, 1))
                a_silu = s_a.inc(); evac_a[bku] = a_silu
                esl = 6 + par
                VE.wait(s_a, a_silu)
                VE.wait(s_t, t_ul)
                VE.wait(s_sp, eslot_sp.get(esl, 0))
                VE.run(lambda e, esl=esl, o=hslot, bkl=bkl: e.tensor_mul(
                    Es(esl), o, PSB[bkl].ap()).then_inc(s_v.h, 1))
                v_hid = s_v.inc()
                hidmul_v[bkl] = v_hid
                hidmul_v[("h", par)] = v_hid
                SY.wait(s_v, v_hid)
                dma(SY, s_sp, hid_sc[hc, 2 * r4], Es(esl)[:, 0:FBT])
                eslot_sp[esl] = dma(SY, s_sp, hid_sc[hc, 2 * r4 + 1], Es(esl)[:, FBT:2 * FBT])
        sp_hid = s_sp.n

        barrier(SY, GP, TE, SC, VE)

        def w2q_view(ldc):
            if ldc < 2:
                return G.ap()[:, ldc * 8192:(ldc + 1) * 8192]
            return Hbf[:, (ldc - 2) * 8192:(ldc - 1) * 8192]

        ep_last_v = 0
        gsl_v = {}
        for qg in range(4):
            SY.wait(s_sp, sp_w2p)
            SY.wait(s_t, s_t.n)
            SY.wait(s_a, s_a.n)
            SY.wait(s_v, ep_last_v)
            cache_ld = 0
            for ldc in range(4):
                dc = qg * 4 + ldc
                for bg in range(8):
                    in_ap = (w2p_sc[bg * 8:(bg + 1) * 8, :, dc * P:(dc + 1) * P]
                             .rearrange("b p q -> p b q"))
                    dst = (w2q_view(ldc)[:, bg * 8 * P:(bg + 1) * 8 * P]
                           .rearrange("p (b q) -> p b q", b=8))
                    cache_ld = dma(SY, s_x, dst, in_ap)
            for tb in range(8):
                r4 = tb // 2
                half = tb % 2
                SY.wait(s_v, ep_last_v)
                SY.wait(s_t, s_t.n if (qg + tb) else 0)
                SY.wait(s_sp, sp_hid)
                hid_ld = 0
                for hcg in range(8):
                    in_ap = hid_sc[hcg * 8:(hcg + 1) * 8, tb].rearrange("h p t -> p h t")
                    dst = (C.ap()[:, hcg * 8 * FBT:(hcg + 1) * 8 * FBT]
                           .rearrange("p (h t) -> p h t", h=8))
                    hid_ld = dma(SY, s_x, dst, in_ap)
                x_in = xT_h.rearrange("(c p) t -> p c t", p=P)[:, :, HALO + tb * FBT:
                                                               HALO + (tb + 1) * FBT]
                x_ld = dma(SY, s_x, Af[:, :, 0:FBT], x_in)
                dma(SY, s_x, SR.ap()[:, 0:FBT], mrs_sc[r4, 0, half * FBT:(half + 1) * FBT])
                m_ld = dma(SY, s_x, SR.ap()[:, TT:TT + FBT],
                           mrs_sc[r4, 1, half * FBT:(half + 1) * FBT])
                ao_ld = 0
                for ldc in range(4):
                    dc = qg * 4 + ldc
                    ao_in = ao_sc[r4, dc][:, half * FBT:(half + 1) * FBT]
                    ao_ld = dma(SY, s_x, E.ap()[:, ldc * FBT:(ldc + 1) * FBT], ao_in)
                TE.wait(s_x, m_ld)
                TE.wait(s_v, ep_last_v)
                TE.run(lambda e: e.matmul(out=PSB[4].ap()[:, 0:FBT], lhsT=ONESR.ap(),
                                          rhs=SR.ap()[:, 0:FBT], start=True, stop=True))
                TE.run(lambda e: e.matmul(out=PSB[5].ap()[:, 0:FBT], lhsT=ONESR.ap(),
                                          rhs=SR.ap()[:, TT:TT + FBT], start=True,
                                          stop=True).then_inc(s_t.h, 1))
                t_bcst = s_t.inc()
                TE.wait(s_x, cache_ld)
                TE.wait(s_x, hid_ld)
                for sem, val in ((s_a, max(evac_a.get(i, 0) for i in range(4))),
                                 (s_v, ep_last_v)):
                    TE.wait(sem, val)
                for bp in range(NH):
                    for ldc in range(4):
                        last = bp == NH - 1 and ldc == 3
                        if last:
                            TE.run(lambda e, bp=bp, ldc=ldc: e.matmul(
                                out=PSB[ldc].ap()[:, 0:FBT],
                                lhsT=w2q_view(ldc)[:, bp * P:(bp + 1) * P],
                                rhs=Cht[:, bp, :], start=False,
                                stop=True).then_inc(s_t.h, 1))
                        else:
                            TE.run(lambda e, bp=bp, ldc=ldc, st=(bp == 0): e.matmul(
                                out=PSB[ldc].ap()[:, 0:FBT],
                                lhsT=w2q_view(ldc)[:, bp * P:(bp + 1) * P],
                                rhs=Cht[:, bp, :], start=st, stop=(bp == NH - 1)))
                t_acc = s_t.inc()
                VE.wait(s_x, x_ld)
                VE.wait(s_x, ao_ld)
                VE.wait(s_t, t_bcst)
                for ldc in range(4):
                    dc = qg * 4 + ldc
                    par = ldc % 2
                    gslt = S.ap()[:, par * FBT:(par + 1) * FBT]
                    xns = S.ap()[:, TT + par * FBT:TT + (par + 1) * FBT]
                    SC.wait(s_t, t_acc)
                    SC.wait(s_v, gsl_v.get(par, 0))
                    SC.run(lambda e, o=gslt, ldc=ldc: e.activation(
                        out=o, in_=PSB[ldc].ap()[:, 0:FBT], func=AF.Silu).then_inc(s_a.h, 1))
                    a_g = s_a.inc()
                    evac_a[ldc] = a_g
                    VE.run(lambda e, o=xns, dc=dc: e.tensor_sub(
                        o, Af[:, dc, 0:FBT], PSB[4].ap()[:, 0:FBT]))
                    VE.run(lambda e, o=xns: e.tensor_mul(o, o, PSB[5].ap()[:, 0:FBT]))
                    VE.run(lambda e, o=xns, ldc=ldc: e.tensor_add(
                        o, o, E.ap()[:, ldc * FBT:(ldc + 1) * FBT]))
                    VE.wait(s_a, a_g)
                    VE.run(lambda e, o=xns, g=gslt: e.tensor_mul(o, o, g))
                    VE.run(lambda e, o=Bf[:, dc, 0:FBT], dc=dc, xns=xns: e.tensor_add(
                        o, Af[:, dc, 0:FBT], xns).then_inc(s_v.h, 1))
                    v_out = s_v.inc()
                    gsl_v[par] = v_out
                    ep_last_v = v_out
                    GP.wait(s_v, v_out)
                    o_ap = out_h.rearrange("(c p) t -> p c t", p=P)[:, dc,
                                                                    tb * FBT:(tb + 1) * FBT]
                    dma(GP, s_sp, o_ap, Bf[:, dc, 0:FBT])

        SY.run(lambda e, v=s_sp.n: e.wait_ge(s_sp.h, v))
        SY.run(lambda e, v=s_x.n: e.wait_ge(s_x.h, v))
        GP.run(lambda e, v=s_g.n: e.wait_ge(s_g.h, v))

        @block.sync
        def _(e):
            for fn in SY.steps:
                fn(e)

        @block.gpsimd
        def _(e):
            for fn in GP.steps:
                fn(e)

        @block.scalar
        def _(e):
            for fn in SC.steps:
                fn(e)

        @block.vector
        def _(e):
            for fn in VE.steps:
                fn(e)

        @block.tensor
        def _(e):
            for fn in TE.steps:
                fn(e)

    return nc


def _hadamard(n):
    h = np.array([[1.0]], np.float32)
    while h.shape[0] < n:
        h = np.block([[h, h], [h, -h]]).astype(np.float32)
    return h


_NC_CACHE = {}


def kernel(L0_attn_x, L0_attn_ln_scale, L0_attn_ln_bias, L0_attn_mix_k,
           L0_attn_mix_v, L0_attn_mix_r, L0_attn_time_first, L0_attn_time_decay,
           L0_attn_Wk, L0_attn_Wv, L0_attn_Wr, L0_attn_Wo,
           L0_ffn_W1, L0_ffn_W2, L0_ffn_W3):
    X = np.asarray(L0_attn_x, np.float32)

    def pack_vec(v):
        return np.ascontiguousarray(np.asarray(v, np.float32).reshape(NCC, P).T)

    vecs = np.ascontiguousarray(np.stack(
        [pack_vec(L0_attn_ln_scale), pack_vec(L0_attn_ln_bias),
         pack_vec(L0_attn_mix_k), pack_vec(L0_attn_mix_v),
         pack_vec(L0_attn_mix_r), pack_vec(L0_attn_time_first),
         pack_vec(L0_attn_time_decay)], axis=1))

    def tile_sq(W):
        a = np.asarray(W, np.float32).reshape(NCC, P, NCC, P)
        return np.ascontiguousarray(a.transpose(2, 1, 0, 3))

    def tile_w13(W):
        a = np.asarray(W, np.float32).reshape(NH, P, NCC, P)
        return np.ascontiguousarray(a.transpose(0, 3, 2, 1))

    def tile_w2(W):
        a = np.asarray(W, np.float32).reshape(NCC, P, NH, P)
        return np.ascontiguousarray(a.transpose(2, 3, 0, 1))

    wk = tile_sq(L0_attn_Wk); wv = tile_sq(L0_attn_Wv)
    wr = tile_sq(L0_attn_Wr); wo = tile_sq(L0_attn_Wo)
    w1 = tile_w13(L0_ffn_W1); w3 = tile_w13(L0_ffn_W3)
    w2 = tile_w2(L0_ffn_W2)
    h128 = _hadamard(128)
    h64 = _hadamard(64)
    ones = np.ones((P, 1), np.float32)
    onesr = np.ones((1, P), np.float32)

    in_maps = []
    for c in range(8):
        b, pos = divmod(c, 4)
        t0 = pos * TREAL
        xT = np.zeros((2048, HALO + TREAL), np.float32)
        src = X[b, max(0, t0 - HALO):t0 + TREAL]
        xT[:, (HALO + TREAL) - src.shape[0]:] = src.T
        pm = np.full((P, 1), 0.0 if pos == 0 else 1.0, np.float32)
        in_maps.append(dict(
            xT=np.ascontiguousarray(xT), vecs=vecs, pmask=pm, ones=ones,
            onesr=onesr, h128=h128, h64=h64, wk=wk, wv=wv, wr=wr, wo=wo,
            w1=w1, w3=w3, w2=w2))

    if "nc" not in _NC_CACHE:
        _NC_CACHE["nc"] = build_program()
    nc = _NC_CACHE["nc"]
    res = run_bass_kernel_spmd(nc, in_maps, core_ids=list(range(8)))

    out = np.empty((2, 8192, 2048), np.float32)
    for c in range(8):
        b, pos = divmod(c, 4)
        out[b, pos * TREAL:(pos + 1) * TREAL] = np.asarray(res.results[c]["outT"]).T
    return out


# version 13
# speedup vs baseline: 1.3948x; 1.0085x over previous
import contextlib
import numpy as np
import concourse.bass as bass
import concourse.mybir as mybir
from concourse.bass_utils import run_bass_kernel_spmd

F32 = mybir.dt.float32
BF16 = mybir.dt.bfloat16
AF = mybir.ActivationFunctionType
OP = mybir.AluOpType

P = 128
NCC = 16
TT = 512
NT = 5
TREAL = 2048
HALO = 512
NH = 64
FBT = 256
LN_EPS = 1e-5
FWHT_SCALE = float(np.float32(1.0 / np.sqrt(np.float64(8192.0))))


class Sem:
    def __init__(self, handle, name):
        self.h = handle
        self.name = name
        self.n = 0

    def inc(self, k=1):
        self.n += k
        return self.n


class Eng:
    def __init__(self, name):
        self.name = name
        self.steps = []
        self.lw = {}

    def wait(self, sem, val):
        if val <= 0 or self.lw.get(sem.name, 0) >= val:
            return
        self.lw[sem.name] = val
        self.steps.append(lambda e, s=sem, v=val: e.wait_ge(s.h, v))

    def run(self, fn):
        self.steps.append(fn)


def build_program():
    nc = bass.Bass("TRN2", target_bir_lowering=False)

    xT_h = nc.dram_tensor("xT", [2048, 2560], F32, kind="ExternalInput").ap()
    vec_h = nc.dram_tensor("vecs", [P, 7, NCC], F32, kind="ExternalInput").ap()
    pm_h = nc.dram_tensor("pmask", [P, 1], F32, kind="ExternalInput").ap()
    ones_h = nc.dram_tensor("ones", [P, 1], F32, kind="ExternalInput").ap()
    onesr_h = nc.dram_tensor("onesr", [1, P], F32, kind="ExternalInput").ap()
    h128_h = nc.dram_tensor("h128", [P, P], F32, kind="ExternalInput").ap()
    h64_h = nc.dram_tensor("h64", [64, 64], F32, kind="ExternalInput").ap()
    wk_h = nc.dram_tensor("wk", [NCC, P, NCC, P], F32, kind="ExternalInput").ap()
    wv_h = nc.dram_tensor("wv", [NCC, P, NCC, P], F32, kind="ExternalInput").ap()
    wr_h = nc.dram_tensor("wr", [NCC, P, NCC, P], F32, kind="ExternalInput").ap()
    wo_h = nc.dram_tensor("wo", [NCC, P, NCC, P], F32, kind="ExternalInput").ap()
    w1_h = nc.dram_tensor("w1", [NH, P, NCC, P], F32, kind="ExternalInput").ap()
    w3_h = nc.dram_tensor("w3", [NH, P, NCC, P], F32, kind="ExternalInput").ap()
    w2_h = nc.dram_tensor("w2", [NH, P, NCC, P], F32, kind="ExternalInput").ap()
    out_h = nc.dram_tensor("outT", [2048, 2048], F32, kind="ExternalOutput").ap()

    wq_sc = nc.dram_tensor("wq_sc", [4, NCC, P, NCC * P], BF16, kind="Internal").ap()
    xnb_sc = nc.dram_tensor("xnb_sc", [4, NCC, P, TT], BF16, kind="Internal").ap()
    rwkv_sc = nc.dram_tensor("rwkv_sc", [4, NCC, P, TT], BF16, kind="Internal").ap()
    ao_sc = nc.dram_tensor("ao_sc", [4, NCC, P, TT], BF16, kind="Internal").ap()
    mrs_sc = nc.dram_tensor("mrs_sc", [4, 2, TT], F32, kind="Internal").ap()
    w2a_sc = nc.dram_tensor("w2a_sc", [NH, P, 2048], BF16, kind="Internal").ap()
    w2p_sc = nc.dram_tensor("w2p_sc", [NH, P, 2048], BF16, kind="Internal").ap()
    hid_sc = nc.dram_tensor("hid_sc", [NH, 8, P, FBT], BF16, kind="Internal").ap()

    with contextlib.ExitStack() as es:
        ec = es.enter_context
        A = ec(nc.sbuf_tensor("A", [P, 16 * TT], F32))
        B = ec(nc.sbuf_tensor("B", [P, 16 * TT], F32))
        C = ec(nc.sbuf_tensor("C", [P, 3 * 16 * TT], BF16))
        G = ec(nc.sbuf_tensor("G", [P, 14336], BF16))
        H = ec(nc.sbuf_tensor("H", [P, 17 * TT], F32))
        E = ec(nc.sbuf_tensor("E", [P, 3072], BF16))
        S = ec(nc.sbuf_tensor("S", [P, 1024], F32))
        RW = ec(nc.sbuf_tensor("RW", [P, NCC * TT], BF16))
        ABu = ec(nc.sbuf_tensor("ABu", [P, TT + 1], F32))
        BBu = ec(nc.sbuf_tensor("BBu", [P, TT + 1], F32))
        VEC = ec(nc.sbuf_tensor("VEC", [P, 7, NCC], F32))
        DER = ec(nc.sbuf_tensor("DER", [P, 64], F32))
        PM = ec(nc.sbuf_tensor("PMt", [P, 1], F32))
        ONES = ec(nc.sbuf_tensor("ONESt", [P, 1], F32))
        ONESR = ec(nc.sbuf_tensor("ONESRt", [1, P], F32))
        H128B = ec(nc.sbuf_tensor("H128Bt", [P, P], BF16))
        H64B = ec(nc.sbuf_tensor("H64Bt", [64, 64], BF16))
        CA = ec(nc.sbuf_tensor("CAt", [P, NCC], F32))
        CB = ec(nc.sbuf_tensor("CBt", [P, NCC], F32))
        CX = ec(nc.sbuf_tensor("CXt", [P, NCC], F32))
        PSB = [ec(nc.psum_tensor(f"PSB{i}", [P, TT], F32)) for i in range(8)]

        s_x = Sem(ec(nc.semaphore("s_x")), "s_x")
        s_g = Sem(ec(nc.semaphore("s_g")), "s_g")
        s_sp = Sem(ec(nc.semaphore("s_sp")), "s_sp")
        s_t = Sem(ec(nc.semaphore("s_t")), "s_t")
        s_v = Sem(ec(nc.semaphore("s_v")), "s_v")
        s_a = Sem(ec(nc.semaphore("s_a")), "s_a")
        block = ec(nc.Block())

        SY = Eng("sync"); GP = Eng("gpsimd"); SC = Eng("scalar")
        VE = Eng("vector"); TE = Eng("tensor")
        ALL_ENG = (SY, GP, SC, VE, TE)

        def barrier(*engs):
            for eng in engs:
                for sem in (s_t, s_v, s_a, s_x, s_g, s_sp):
                    eng.wait(sem, sem.n)

        Af = A.ap().rearrange("p (c t) -> p c t", c=16)
        Bf = B.ap().rearrange("p (c t) -> p c t", c=16)
        Abf = A.ap().bitcast(BF16)
        Bbf = B.ap().bitcast(BF16)
        Hbf = H.ap().bitcast(BF16)
        Cm = C.ap().rearrange("p (q c t) -> p q c t", q=3, c=16)
        Cht = C.ap().rearrange("p (h t) -> p h t", t=FBT)

        def Hs(i):
            return H.ap()[:, i * TT:(i + 1) * TT]

        def Es(i):
            return E.ap()[:, i * TT:(i + 1) * TT]

        def Gs(i):
            return G.ap()[:, i * 2048:(i + 1) * 2048]

        EWc = DER.ap()[:, 16:32]
        EUc = DER.ap()[:, 32:48]

        def vslice(j, cc):
            return VEC.ap()[:, j, cc:cc + 1]

        def dma(ENG, sem, out, in_):
            ENG.run(lambda e, o=out, i=in_, s=sem: e.dma_start(out=o, in_=i).then_inc(s.h, 16))
            return sem.inc(16)

        def mm_acc(bank, lhsT_list, rhs_list, waits=()):
            n = len(lhsT_list)
            for sem, val in waits:
                TE.wait(sem, val)
            for i in range(n):
                last = i == n - 1
                lt, rh = lhsT_list[i], rhs_list[i]
                if last:
                    TE.run(lambda e, lt=lt, rh=rh, b=bank, st=(i == 0): e.matmul(
                        out=PSB[b].ap(), lhsT=lt, rhs=rh, start=st,
                        stop=True).then_inc(s_t.h, 1))
                else:
                    TE.run(lambda e, lt=lt, rh=rh, b=bank, st=(i == 0): e.matmul(
                        out=PSB[b].ap(), lhsT=lt, rhs=rh, start=st, stop=False))
            return s_t.inc()

        evac_a = {}
        psum_v = {}
        sq_mm = {}
        gslot_t = {}
        gslot_v = {}
        eslot_sp = {}

        n_consts = 0
        for out, in_ in ((VEC.ap(), vec_h), (PM.ap(), pm_h), (ONES.ap(), ones_h),
                         (ONESR.ap(), onesr_h)):
            n_consts = dma(SY, s_x, out, in_)
        n_h = 0
        for out, in_ in ((H128B.ap(), h128_h), (H64B.ap(), h64_h)):
            n_h = dma(GP, s_g, out, in_)

        SC.wait(s_x, n_consts)
        SC.run(lambda e: e.activation(out=DER.ap()[:, 0:16], in_=VEC.ap()[:, 6, :],
                                      func=AF.Exp).then_inc(s_a.h, 1))
        a0 = s_a.inc()
        VE.wait(s_x, n_consts)
        VE.wait(s_a, a0)
        VE.run(lambda e: e.tensor_single_scalar(out=DER.ap()[:, 0:16], in_=DER.ap()[:, 0:16],
                                                scalar=-1.0, op=OP.mult).then_inc(s_v.h, 1))
        v0 = s_v.inc()
        SC.wait(s_v, v0)
        SC.run(lambda e: e.activation(out=DER.ap()[:, 16:32], in_=DER.ap()[:, 0:16], func=AF.Exp))
        SC.run(lambda e: e.activation(out=DER.ap()[:, 32:48], in_=VEC.ap()[:, 5, :],
                                      func=AF.Exp).then_inc(s_a.h, 1))
        a_der = s_a.inc()
        VE.run(lambda e: e.memset(CA.ap(), 0.0))
        VE.run(lambda e: e.memset(CB.ap(), 0.0))
        VE.run(lambda e: e.memset(CX.ap(), 0.0).then_inc(s_v.h, 1))
        s_v.inc()

        wqh = (wk_h, wv_h, wr_h, wo_h)
        cast_ready = {}
        for mc in range(NCC):
            for q in range(4):
                GP.wait(s_g, s_g.n)
                g_ld = dma(GP, s_g, Gs(6), wqh[q][mc].rearrange("p c q -> p (c q)"))
                GP.wait(s_g, g_ld)
                cast_ready[(q, mc)] = dma(GP, s_g, wq_sc[q, mc], Gs(6))
        g_castpass = s_g.n

        SQSL = (0, 1)
        KF = (2, 3); VF = (4, 5); RF = (6, 7)
        EKs, EKVs, T1s, T2s, NUMs, DENs, RECs = 8, 9, 10, 11, 12, 13, 16
        XN2s, PREVs, TMPs = 14, 15, 16

        mrow = S.ap()[0:1, 0:TT]; rrow = S.ap()[0:1, TT:2 * TT]

        last_norm_v = 0
        last_stat_v = 0
        mrs_spill = 0
        v_mixes = 0
        kv_mm_done = {}
        scan_v = {}
        scan_v_prev = 0
        wo_slot_t = {}
        wo_end_t = 0

        for tau in range(NT):
            real = tau >= 1
            r = tau - 1

            xre = xT_h.rearrange("(c p) t -> p c t", p=P)
            SY.wait(s_v, last_norm_v)
            x_ld = 0
            for ccg in range(4):
                x_ld = dma(SY, s_x, Af[:, ccg * 4:(ccg + 1) * 4, :],
                           xre[:, ccg * 4:(ccg + 1) * 4, TT * tau: TT * (tau + 1)])

            ln_mm = 0
            for cc in range(NCC):
                sl = SQSL[cc % 2]
                sq = Hs(sl)
                SC.wait(s_x, x_ld)
                SC.wait(s_t, sq_mm.get(sl, 0))
                SC.run(lambda e, o=sq, i=Af[:, cc, :]: e.activation(
                    out=o, in_=i, func=AF.Square).then_inc(s_a.h, 1))
                a_sq = s_a.inc()
                TE.wait(s_x, x_ld)
                TE.wait(s_v, last_stat_v)
                TE.run(lambda e, i=Af[:, cc, :], st=(cc == 0): e.matmul(
                    out=PSB[0].ap()[0:1, :], lhsT=ONES.ap(), rhs=i, start=st,
                    stop=(cc == NCC - 1)))
                TE.wait(s_a, a_sq)
                TE.run(lambda e, i2=sq, st=(cc == 0): e.matmul(
                    out=PSB[1].ap()[0:1, :], lhsT=ONES.ap(), rhs=i2, start=st,
                    stop=(cc == NCC - 1)).then_inc(s_t.h, 1))
                ln_mm = s_t.inc()
                sq_mm[sl] = ln_mm

            VE.wait(s_t, ln_mm)
            VE.wait(s_sp, mrs_spill)
            VE.run(lambda e: e.tensor_single_scalar(out=mrow, in_=PSB[0].ap()[0:1, :],
                                                    scalar=1.0 / 2048.0, op=OP.mult))
            VE.run(lambda e: e.tensor_single_scalar(out=rrow, in_=PSB[1].ap()[0:1, :],
                                                    scalar=1.0 / 2048.0, op=OP.mult))
            VE.run(lambda e: e.tensor_mul(PSB[0].ap()[0:1, :], mrow, mrow))
            VE.run(lambda e: e.tensor_sub(rrow, rrow, PSB[0].ap()[0:1, :]))
            VE.run(lambda e: e.tensor_single_scalar(out=rrow, in_=rrow, scalar=LN_EPS,
                                                    op=OP.add).then_inc(s_v.h, 1))
            last_stat_v = s_v.inc()
            SC.wait(s_v, last_stat_v)
            SC.run(lambda e: e.activation(out=rrow, in_=rrow, func=AF.Ln))
            SC.run(lambda e: e.activation(out=rrow, in_=rrow, func=AF.Exp,
                                          scale=-0.5).then_inc(s_a.h, 1))
            a_rstd = s_a.inc()
            if real:
                SC.wait(s_a, a_rstd)
                dma(SC, s_sp, mrs_sc[r, 0], mrow)
                mrs_spill = dma(SC, s_sp, mrs_sc[r, 1], rrow)
            TE.wait(s_a, a_rstd)
            TE.wait(s_v, last_norm_v)
            TE.wait(s_a, evac_a.get(2, 0))
            TE.wait(s_a, evac_a.get(3, 0))
            TE.run(lambda e: e.matmul(out=PSB[2].ap(), lhsT=ONESR.ap(), rhs=mrow,
                                      start=True, stop=True))
            TE.run(lambda e: e.matmul(out=PSB[3].ap(), lhsT=ONESR.ap(), rhs=rrow,
                                      start=True, stop=True).then_inc(s_t.h, 1))
            t_bc = s_t.inc()

            VE.wait(s_t, t_bc)
            for cc in range(NCC):
                VE.run(lambda e, cc=cc: e.tensor_sub(Bf[:, cc, :], Af[:, cc, :], PSB[2].ap()))
                VE.run(lambda e, cc=cc: e.tensor_mul(Bf[:, cc, :], Bf[:, cc, :], PSB[3].ap()))
                if real:
                    sl = cc % 2
                    VE.wait(s_sp, eslot_sp.get(sl, 0))
                    VE.run(lambda e, cc=cc, sl=sl: e.tensor_copy(
                        out=Es(sl), in_=Bf[:, cc, :]).then_inc(s_v.h, 1))
                    v_xnb = s_v.inc()
                    SC.wait(s_v, v_xnb)
                    eslot_sp[sl] = dma(SC, s_sp, xnb_sc[r, cc], Es(sl))
            VE.run(lambda e: e.engine_nop().then_inc(s_v.h, 1))
            last_norm_v = s_v.inc()

            ln_mm2 = 0
            for cc in range(NCC):
                sl = SQSL[cc % 2]
                sq = Hs(sl)
                SC.wait(s_v, last_norm_v)
                SC.wait(s_t, sq_mm.get(sl, 0))
                SC.run(lambda e, o=sq, i=Bf[:, cc, :]: e.activation(
                    out=o, in_=i, func=AF.Square).then_inc(s_a.h, 1))
                a_sq = s_a.inc()
                TE.wait(s_v, last_stat_v)
                TE.wait(s_v, last_norm_v)
                TE.run(lambda e, i=Bf[:, cc, :], st=(cc == 0): e.matmul(
                    out=PSB[0].ap()[0:1, :], lhsT=ONES.ap(), rhs=i, start=st,
                    stop=(cc == NCC - 1)))
                TE.wait(s_a, a_sq)
                TE.run(lambda e, i2=sq, st=(cc == 0): e.matmul(
                    out=PSB[1].ap()[0:1, :], lhsT=ONES.ap(), rhs=i2, start=st,
                    stop=(cc == NCC - 1)).then_inc(s_t.h, 1))
                ln_mm2 = s_t.inc()
                sq_mm[sl] = ln_mm2

            VE.wait(s_t, ln_mm2)
            VE.wait(s_sp, mrs_spill)
            VE.run(lambda e: e.tensor_single_scalar(out=mrow, in_=PSB[0].ap()[0:1, :],
                                                    scalar=1.0 / 2048.0, op=OP.mult))
            VE.run(lambda e: e.tensor_single_scalar(out=rrow, in_=PSB[1].ap()[0:1, :],
                                                    scalar=1.0 / 2048.0, op=OP.mult))
            VE.run(lambda e: e.tensor_mul(PSB[0].ap()[0:1, :], mrow, mrow))
            VE.run(lambda e: e.tensor_sub(rrow, rrow, PSB[0].ap()[0:1, :]))
            VE.run(lambda e: e.tensor_single_scalar(out=rrow, in_=rrow, scalar=LN_EPS,
                                                    op=OP.add).then_inc(s_v.h, 1))
            last_stat_v = s_v.inc()
            SC.wait(s_v, last_stat_v)
            SC.run(lambda e: e.activation(out=rrow, in_=rrow, func=AF.Ln))
            SC.run(lambda e: e.activation(out=rrow, in_=rrow, func=AF.Exp,
                                          scale=-0.5).then_inc(s_a.h, 1))
            a_rstd2 = s_a.inc()
            TE.wait(s_a, a_rstd2)
            TE.wait(s_v, v_mixes)
            TE.wait(s_a, evac_a.get(2, 0))
            TE.wait(s_a, evac_a.get(3, 0))
            TE.run(lambda e: e.matmul(out=PSB[2].ap(), lhsT=ONESR.ap(), rhs=mrow,
                                      start=True, stop=True))
            TE.run(lambda e: e.matmul(out=PSB[3].ap(), lhsT=ONESR.ap(), rhs=rrow,
                                      start=True, stop=True).then_inc(s_t.h, 1))
            t_bc2 = s_t.inc()

            xn2 = Hs(XN2s); prev = Hs(PREVs); tmp = Hs(TMPs)
            VE.wait(s_t, t_bc2)
            VE.wait(s_t, wo_end_t)
            nq = 3 if real else 2
            for cc in range(NCC):
                VE.run(lambda e, cc=cc, o=xn2: e.tensor_sub(o, Bf[:, cc, :], PSB[2].ap()))
                VE.run(lambda e, o=xn2: e.tensor_mul(o, o, PSB[3].ap()))
                VE.run(lambda e, o=xn2, cc=cc: e.tensor_scalar(
                    out=o, in0=o, scalar1=vslice(0, cc), scalar2=vslice(1, cc),
                    op0=OP.mult, op1=OP.add))
                VE.run(lambda e, o=prev, i=xn2: e.tensor_copy(out=o[:, 1:TT], in_=i[:, 0:TT - 1]))
                if tau == 1:
                    VE.run(lambda e, o=prev, cc=cc: e.tensor_single_scalar(
                        out=o[:, 0:1], in_=CX.ap()[:, cc:cc + 1], scalar=PM.ap(), op=OP.mult))
                else:
                    VE.run(lambda e, o=prev, cc=cc: e.tensor_copy(
                        out=o[:, 0:1], in_=CX.ap()[:, cc:cc + 1]))
                VE.run(lambda e, i=xn2, cc=cc: e.tensor_copy(
                    out=CX.ap()[:, cc:cc + 1], in_=i[:, TT - 1:TT]))
                VE.run(lambda e, o=tmp, i=xn2, pr=prev: e.tensor_sub(o, i, pr))
                for q in range(nq):
                    VE.run(lambda e, q=q, cc=cc, t=tmp, pr=prev: e.scalar_tensor_tensor(
                        out=Cm[:, q, cc, :], in0=t, scalar=vslice(2 + q, cc), in1=pr,
                        op0=OP.mult, op1=OP.add))
            VE.run(lambda e: e.engine_nop().then_inc(s_v.h, 1))
            v_mixes = s_v.inc()

            def load_strips(oc2):
                lds = []
                for q in range(nq):
                    slot = q * 2 + oc2 % 2
                    SY.wait(s_g, cast_ready[(q, oc2)])
                    SY.wait(s_t, kv_mm_done.get(slot, 0))
                    lds.append(dma(SY, s_x, Gs(slot), wq_sc[q, oc2]))
                return lds
            strip_pending = {0: load_strips(0)}
            for oc in range(NCC):
                banks = (4, 5, 6) if oc % 2 == 0 else (7, 2, 3)
                strip_ld = strip_pending.pop(oc)
                if oc + 1 < NCC:
                    strip_pending[oc + 1] = load_strips(oc + 1)
                if oc < 2:
                    TE.wait(s_v, v_mixes)
                mmq = []
                for q in range(nq):
                    slot = q * 2 + oc % 2
                    bank = banks[q]
                    lt = [Gs(slot)[:, ic * P:(ic + 1) * P] for ic in range(NCC)]
                    rh = [Cm[:, q, ic, :] for ic in range(NCC)]
                    t_mm = mm_acc(bank, lt, rh, waits=(
                        (s_x, strip_ld[q]), (s_v, v_mixes),
                        (s_a, evac_a.get(bank, 0)),
                        (s_v, psum_v.get(bank, 0))))
                    mmq.append(t_mm)
                    kv_mm_done[slot] = t_mm

                kf = Hs(KF[oc % 2]); vf = Hs(VF[oc % 2]); rf = Hs(RF[oc % 2])
                SC.wait(s_t, mmq[0])
                SC.wait(s_v, scan_v.get(oc % 2, 0))
                SC.run(lambda e, o=kf, b=banks[0]: e.activation(
                    out=o, in_=PSB[b].ap(), func=AF.Copy).then_inc(s_a.h, 1))
                evac_a[banks[0]] = s_a.inc()
                SC.wait(s_t, mmq[1])
                SC.run(lambda e, o=vf, b=banks[1]: e.activation(
                    out=o, in_=PSB[b].ap(), func=AF.Copy).then_inc(s_a.h, 1))
                evac_a[banks[1]] = s_a.inc()
                if real:
                    SC.wait(s_t, mmq[2])
                    SC.run(lambda e, o=rf, b=banks[2]: e.activation(
                        out=o, in_=PSB[b].ap(), func=AF.Sigmoid).then_inc(s_a.h, 1))
                    evac_a[banks[2]] = s_a.inc()
                SC.wait(s_v, scan_v_prev)
                SC.run(lambda e, o=Hs(EKs), i=kf: e.activation(
                    out=o, in_=i, func=AF.Exp).then_inc(s_a.h, 1))
                a_ek = s_a.inc()

                ek = Hs(EKs); ekv = Hs(EKVs); t1 = Hs(T1s); t2 = Hs(T2s)
                num = Hs(NUMs); den = Hs(DENs); rec = Hs(RECs)
                VE.wait(s_a, a_ek)
                VE.run(lambda e, cc=oc: e.tensor_single_scalar(
                    out=t2, in_=ek, scalar=EUc[:, cc:cc + 1], op=OP.mult))
                VE.run(lambda e, vf=vf: e.tensor_mul(t1, t2, vf))
                VE.run(lambda e, vf=vf: e.tensor_mul(ekv, ek, vf))
                if tau == 1:
                    VE.run(lambda e, cc=oc: e.tensor_single_scalar(
                        out=ABu.ap()[:, 0:1], in_=CA.ap()[:, cc:cc + 1], scalar=PM.ap(),
                        op=OP.mult))
                    VE.run(lambda e, cc=oc: e.tensor_single_scalar(
                        out=BBu.ap()[:, 0:1], in_=CB.ap()[:, cc:cc + 1], scalar=PM.ap(),
                        op=OP.mult))
                    VE.run(lambda e, cc=oc: e.scalar_tensor_tensor(
                        out=ABu.ap()[:, 1:2], in0=ABu.ap()[:, 0:1], scalar=EWc[:, cc:cc + 1],
                        in1=ekv[:, 0:1], op0=OP.mult, op1=OP.add))
                    VE.run(lambda e, cc=oc: e.scalar_tensor_tensor(
                        out=BBu.ap()[:, 1:2], in0=BBu.ap()[:, 0:1], scalar=EWc[:, cc:cc + 1],
                        in1=ek[:, 0:1], op0=OP.mult, op1=OP.add))
                    VE.run(lambda e: e.engine_nop())
                    VE.run(lambda e: e.engine_nop())
                    VE.run(lambda e, cc=oc: e.tensor_tensor_scan(
                        out=ABu.ap()[:, 2:TT + 1],
                        data0=EWc[:, cc:cc + 1].broadcast_to([P, TT - 1]),
                        data1=ekv[:, 1:TT], initial=ABu.ap()[:, 1:2],
                        op0=OP.mult, op1=OP.add))
                    VE.run(lambda e, cc=oc: e.tensor_tensor_scan(
                        out=BBu.ap()[:, 2:TT + 1],
                        data0=EWc[:, cc:cc + 1].broadcast_to([P, TT - 1]),
                        data1=ek[:, 1:TT], initial=BBu.ap()[:, 1:2],
                        op0=OP.mult, op1=OP.add))
                else:
                    VE.run(lambda e, cc=oc: e.tensor_copy(out=ABu.ap()[:, 0:1],
                                                          in_=CA.ap()[:, cc:cc + 1]))
                    VE.run(lambda e, cc=oc: e.tensor_copy(out=BBu.ap()[:, 0:1],
                                                          in_=CB.ap()[:, cc:cc + 1]))
                    VE.run(lambda e: e.engine_nop())
                    VE.run(lambda e: e.engine_nop())
                    VE.run(lambda e, cc=oc: e.tensor_tensor_scan(
                        out=ABu.ap()[:, 1:TT + 1],
                        data0=EWc[:, cc:cc + 1].broadcast_to([P, TT]),
                        data1=ekv, initial=ABu.ap()[:, 0:1], op0=OP.mult, op1=OP.add))
                    VE.run(lambda e, cc=oc: e.tensor_tensor_scan(
                        out=BBu.ap()[:, 1:TT + 1],
                        data0=EWc[:, cc:cc + 1].broadcast_to([P, TT]),
                        data1=ek, initial=BBu.ap()[:, 0:1], op0=OP.mult, op1=OP.add))
                VE.run(lambda e, cc=oc: e.tensor_copy(out=CA.ap()[:, cc:cc + 1],
                                                      in_=ABu.ap()[:, TT:TT + 1]))
                VE.run(lambda e, cc=oc: e.tensor_copy(out=CB.ap()[:, cc:cc + 1],
                                                      in_=BBu.ap()[:, TT:TT + 1]))
                if real:
                    VE.run(lambda e: e.tensor_add(num, ABu.ap()[:, 0:TT], t1))
                    VE.run(lambda e: e.tensor_add(den, BBu.ap()[:, 0:TT], t2))
                    VE.run(lambda e: e.reciprocal(rec, den))
                    VE.run(lambda e: e.tensor_mul(num, num, rec))
                    VE.run(lambda e, oc=oc, rf=rf: e.tensor_mul(RW.ap()[:, oc * TT:(oc + 1) * TT], rf, num))
                VE.run(lambda e: e.engine_nop().then_inc(s_v.h, 1))
                scan_v[oc % 2] = scan_v_prev = s_v.inc()

            if real:
                def load_wo(mc2):
                    slot = mc2 % 2
                    SY.wait(s_g, cast_ready[(3, mc2)])
                    SY.wait(s_t, kv_mm_done.get(slot, 0))
                    SY.wait(s_t, wo_slot_t.get(slot, 0))
                    return dma(SY, s_x, Gs(slot), wq_sc[3, mc2])
                wo_pending = {0: load_wo(0)}
                TE.wait(s_v, scan_v_prev)
                for mc in range(NCC):
                    slot = mc % 2
                    wo_ld = wo_pending.pop(mc)
                    if mc + 1 < NCC:
                        wo_pending[mc + 1] = load_wo(mc + 1)
                    par = mc % 2
                    bank = 4 + par
                    lt = [Gs(slot)[:, kc * P:(kc + 1) * P] for kc in range(NCC)]
                    rh = [RW.ap()[:, kc * TT:(kc + 1) * TT] for kc in range(NCC)]
                    t_mm = mm_acc(bank, lt, rh, waits=(
                        (s_x, wo_ld), (s_a, evac_a.get(bank, 0))))
                    wo_slot_t[slot] = t_mm
                    kv_mm_done[slot] = t_mm
                    esl = 4 + par
                    SC.wait(s_t, t_mm)
                    SC.wait(s_sp, eslot_sp.get(esl, 0))
                    SC.run(lambda e, esl=esl, bank=bank: e.activation(
                        out=Es(esl), in_=PSB[bank].ap(), func=AF.Copy).then_inc(s_a.h, 1))
                    a_ev = s_a.inc(); evac_a[bank] = a_ev
                    SC.wait(s_a, a_ev)
                    eslot_sp[esl] = dma(SC, s_sp, ao_sc[r, mc], Es(esl))
                wo_end_t = s_t.n

        barrier(GP, TE, SC, SY)
        GP.wait(s_g, n_h)
        st1_slot_t = {}
        def load_w2(b2):
            slot = b2 % 4
            GP.wait(s_t, st1_slot_t.get(slot, 0))
            return dma(GP, s_g, Gs(slot), w2_h[b2].rearrange("p c q -> p (c q)"))
        w2_pending = {0: load_w2(0), 1: load_w2(1)}
        for b in range(NH):
            slot = b % 4
            w2_ld = w2_pending.pop(b)
            if b + 2 < NH:
                w2_pending[b + 2] = load_w2(b + 2)
            for dg in range(4):
                par = (b * 4 + dg) % 4
                bank = 4 + par
                TE.wait(s_g, w2_ld)
                TE.wait(s_a, evac_a.get(bank, 0))
                TE.run(lambda e, slot=slot, dg=dg, bank=bank: e.matmul(
                    out=PSB[bank].ap(), lhsT=H128B.ap(),
                    rhs=Gs(slot)[:, dg * TT:(dg + 1) * TT],
                    start=True, stop=True).then_inc(s_t.h, 1))
                t_mm = s_t.inc(); st1_slot_t[slot] = t_mm
                esl = 2 + par
                SC.wait(s_t, t_mm)
                SC.wait(s_sp, eslot_sp.get(esl, 0))
                SC.run(lambda e, esl=esl, bank=bank: e.activation(
                    out=Es(esl), in_=PSB[bank].ap(), func=AF.Copy).then_inc(s_a.h, 1))
                a_ev = s_a.inc(); evac_a[bank] = a_ev
                SY.wait(s_a, a_ev)
                eslot_sp[esl] = dma(SY, s_sp, w2a_sc[b, :, dg * TT:(dg + 1) * TT], Es(esl))
        sp_w2a = s_sp.n

        w2a_flat = w2a_sc.rearrange("b p f -> b (p f)")
        w2p_flat = w2p_sc.rearrange("b p f -> b (p f)")
        st2_slot_t = {}
        def load_blk(b2):
            slot = b2 % 4
            SY.wait(s_sp, sp_w2a)
            SY.wait(s_t, st2_slot_t.get(slot, 0))
            return dma(SY, s_x, Gs(slot)[0:64, :], w2a_flat[:, b2 * 2048:(b2 + 1) * 2048])
        blk_pending = {0: load_blk(0), 1: load_blk(1)}
        for blk in range(P):
            slot = blk % 4
            r_ld = blk_pending.pop(blk)
            if blk + 2 < P:
                blk_pending[blk + 2] = load_blk(blk + 2)
            for sg in range(4):
                par = (blk * 4 + sg) % 4
                bank = 4 + par
                TE.wait(s_x, r_ld)
                TE.wait(s_a, evac_a.get(bank, 0))
                TE.run(lambda e, slot=slot, sg=sg, bank=bank: e.matmul(
                    out=PSB[bank].ap()[0:64, :], lhsT=H64B.ap(),
                    rhs=Gs(slot)[0:64, sg * TT:(sg + 1) * TT],
                    start=True, stop=True).then_inc(s_t.h, 1))
                t_mm = s_t.inc(); st2_slot_t[slot] = t_mm
                esl = 2 + par
                SC.wait(s_t, t_mm)
                SC.wait(s_sp, eslot_sp.get(esl, 0))
                SC.run(lambda e, esl=esl, bank=bank: e.activation(
                    out=Es(esl)[0:64, :], in_=PSB[bank].ap()[0:64, :], func=AF.Copy,
                    scale=FWHT_SCALE).then_inc(s_a.h, 1))
                a_ev = s_a.inc(); evac_a[bank] = a_ev
                SY.wait(s_a, a_ev)
                o_ap = w2p_flat[:, blk * 2048 + sg * TT: blk * 2048 + (sg + 1) * TT]
                eslot_sp[esl] = dma(SY, s_sp, o_ap, Es(esl)[0:64, :])
        sp_w2p = s_sp.n

        barrier(SY, GP, TE, SC, VE)
        xnb_ld = 0
        for cc in range(NCC):
            for r4 in range(4):
                base = (cc % 8) * 2048 + r4 * TT
                dst = (Abf if cc < 8 else Bbf)[:, base:base + TT]
                xnb_ld = dma(SY, s_x, dst, xnb_sc[r4, cc])

        def xnb_view(dc, r4):
            base = (dc % 8) * 2048 + r4 * TT
            return (Abf if dc < 8 else Bbf)[:, base:base + TT]

        ffa_slot_t = {}
        hidmul_v = {}
        def load_w13(hc2):
            s1 = hc2 % 2; s3 = 2 + hc2 % 2
            GP.wait(s_t, ffa_slot_t.get(s1, 0))
            GP.wait(s_t, ffa_slot_t.get(s3, 0))
            return (dma(GP, s_g, Gs(s1), w1_h[hc2].rearrange("p c q -> p (c q)")),
                    dma(GP, s_g, Gs(s3), w3_h[hc2].rearrange("p c q -> p (c q)")))
        ffa_pending = {0: load_w13(0)}
        for hc in range(NH):
            s1 = hc % 2; s3 = 2 + hc % 2
            w1_ld, w3_ld = ffa_pending.pop(hc)
            if hc + 1 < NH:
                ffa_pending[hc + 1] = load_w13(hc + 1)
            for r4 in range(4):
                par = r4 % 2
                bku = 4 + par * 2
                bkl = 5 + par * 2
                lt1 = [Gs(s1)[:, dc * P:(dc + 1) * P] for dc in range(NCC)]
                lt3 = [Gs(s3)[:, dc * P:(dc + 1) * P] for dc in range(NCC)]
                rh = [xnb_view(dc, r4) for dc in range(NCC)]
                t_ug = mm_acc(bku, lt1, rh, waits=(
                    (s_g, w1_ld), (s_x, xnb_ld), (s_a, evac_a.get(bku, 0))))
                ffa_slot_t[s1] = t_ug
                t_ul = mm_acc(bkl, lt3, rh, waits=(
                    (s_g, w3_ld), (s_v, hidmul_v.get(bkl, 0))))
                ffa_slot_t[s3] = t_ul
                hslot = Hs(SQSL[par])
                SC.wait(s_t, t_ug)
                SC.wait(s_v, hidmul_v.get(("h", par), 0))
                SC.run(lambda e, o=hslot, bku=bku: e.activation(
                    out=o, in_=PSB[bku].ap(), func=AF.Silu).then_inc(s_a.h, 1))
                a_silu = s_a.inc(); evac_a[bku] = a_silu
                esl = par
                VE.wait(s_a, a_silu)
                VE.wait(s_t, t_ul)
                VE.wait(s_sp, eslot_sp.get(esl, 0))
                VE.run(lambda e, esl=esl, o=hslot, bkl=bkl: e.tensor_mul(
                    Es(esl), o, PSB[bkl].ap()).then_inc(s_v.h, 1))
                v_hid = s_v.inc()
                hidmul_v[bkl] = v_hid
                hidmul_v[("h", par)] = v_hid
                SY.wait(s_v, v_hid)
                dma(SY, s_sp, hid_sc[hc, 2 * r4], Es(esl)[:, 0:FBT])
                eslot_sp[esl] = dma(SY, s_sp, hid_sc[hc, 2 * r4 + 1], Es(esl)[:, FBT:2 * FBT])
        sp_hid = s_sp.n

        barrier(SY, GP, TE, SC, VE)

        def w2q_view(ldc):
            if ldc == 0:
                return G.ap()[:, 0:8192]
            if ldc < 3:
                return Hbf[:, (ldc - 1) * 8192:ldc * 8192]
            return C.ap()[:, 16384:24576]

        ep_last_v = 0
        gsl_v = {}
        for qg in range(4):
            SY.wait(s_sp, sp_w2p)
            SY.wait(s_t, s_t.n)
            SY.wait(s_a, s_a.n)
            SY.wait(s_v, ep_last_v)
            cache_ld = 0
            for ldc in range(4):
                dc = qg * 4 + ldc
                for bg in range(8):
                    in_ap = (w2p_sc[bg * 8:(bg + 1) * 8, :, dc * P:(dc + 1) * P]
                             .rearrange("b p q -> p b q"))
                    dst = (w2q_view(ldc)[:, bg * 8 * P:(bg + 1) * 8 * P]
                           .rearrange("p (b q) -> p b q", b=8))
                    cache_ld = dma(SY, s_x, dst, in_ap)
            for tb in range(8):
                r4 = tb // 2
                half = tb % 2
                SY.wait(s_v, ep_last_v)
                SY.wait(s_t, s_t.n if (qg + tb) else 0)
                SY.wait(s_sp, sp_hid)
                hid_ld = 0
                for hcg in range(8):
                    in_ap = hid_sc[hcg * 8:(hcg + 1) * 8, tb].rearrange("h p t -> p h t")
                    dst = (C.ap()[:, hcg * 8 * FBT:(hcg + 1) * 8 * FBT]
                           .rearrange("p (h t) -> p h t", h=8))
                    hid_ld = dma(SY, s_x, dst, in_ap)
                x_in = xT_h.rearrange("(c p) t -> p c t", p=P)[:, :, HALO + tb * FBT:
                                                               HALO + (tb + 1) * FBT]
                x_ld = dma(SY, s_x, Af[:, :, 0:FBT], x_in)
                mrow5 = B.ap()[0:1, FBT:2 * FBT]
                rrow5 = B.ap()[0:1, TT + FBT:TT + 2 * FBT]
                dma(SY, s_x, mrow5, mrs_sc[r4, 0, half * FBT:(half + 1) * FBT])
                m_ld = dma(SY, s_x, rrow5, mrs_sc[r4, 1, half * FBT:(half + 1) * FBT])
                ao_ld = 0
                for ldc in range(4):
                    dc = qg * 4 + ldc
                    ao_in = ao_sc[r4, dc][:, half * FBT:(half + 1) * FBT]
                    ao_ld = dma(SY, s_x, E.ap()[:, ldc * FBT:(ldc + 1) * FBT], ao_in)
                TE.wait(s_x, m_ld)
                TE.wait(s_v, ep_last_v)
                TE.run(lambda e, m=mrow5: e.matmul(out=PSB[4].ap()[:, 0:FBT], lhsT=ONESR.ap(),
                                          rhs=m, start=True, stop=True))
                TE.run(lambda e, rr5=rrow5: e.matmul(out=PSB[5].ap()[:, 0:FBT], lhsT=ONESR.ap(),
                                          rhs=rr5, start=True,
                                          stop=True).then_inc(s_t.h, 1))
                t_bcst = s_t.inc()
                TE.wait(s_x, cache_ld)
                TE.wait(s_x, hid_ld)
                for sem, val in ((s_a, max(evac_a.get(i, 0) for i in range(4))),
                                 (s_v, ep_last_v)):
                    TE.wait(sem, val)
                for bp in range(NH):
                    for ldc in range(4):
                        last = bp == NH - 1 and ldc == 3
                        if last:
                            TE.run(lambda e, bp=bp, ldc=ldc: e.matmul(
                                out=PSB[ldc].ap()[:, 0:FBT],
                                lhsT=w2q_view(ldc)[:, bp * P:(bp + 1) * P],
                                rhs=Cht[:, bp, :], start=False,
                                stop=True).then_inc(s_t.h, 1))
                        else:
                            TE.run(lambda e, bp=bp, ldc=ldc, st=(bp == 0): e.matmul(
                                out=PSB[ldc].ap()[:, 0:FBT],
                                lhsT=w2q_view(ldc)[:, bp * P:(bp + 1) * P],
                                rhs=Cht[:, bp, :], start=st, stop=(bp == NH - 1)))
                t_acc = s_t.inc()
                VE.wait(s_x, x_ld)
                VE.wait(s_x, ao_ld)
                VE.wait(s_t, t_bcst)
                for ldc in range(4):
                    dc = qg * 4 + ldc
                    par = ldc % 2
                    gslt = S.ap()[:, par * FBT:(par + 1) * FBT]
                    xns = S.ap()[:, TT + par * FBT:TT + (par + 1) * FBT]
                    SC.wait(s_t, t_acc)
                    SC.wait(s_v, gsl_v.get(par, 0))
                    SC.run(lambda e, o=gslt, ldc=ldc: e.activation(
                        out=o, in_=PSB[ldc].ap()[:, 0:FBT], func=AF.Silu).then_inc(s_a.h, 1))
                    a_g = s_a.inc()
                    evac_a[ldc] = a_g
                    VE.run(lambda e, o=xns, dc=dc: e.tensor_sub(
                        o, Af[:, dc, 0:FBT], PSB[4].ap()[:, 0:FBT]))
                    VE.run(lambda e, o=xns: e.tensor_mul(o, o, PSB[5].ap()[:, 0:FBT]))
                    VE.run(lambda e, o=xns, ldc=ldc: e.tensor_add(
                        o, o, E.ap()[:, ldc * FBT:(ldc + 1) * FBT]))
                    VE.wait(s_a, a_g)
                    VE.run(lambda e, o=xns, g=gslt: e.tensor_mul(o, o, g))
                    VE.run(lambda e, o=Bf[:, dc, 0:FBT], dc=dc, xns=xns: e.tensor_add(
                        o, Af[:, dc, 0:FBT], xns).then_inc(s_v.h, 1))
                    v_out = s_v.inc()
                    gsl_v[par] = v_out
                    ep_last_v = v_out
                    GP.wait(s_v, v_out)
                    o_ap = out_h.rearrange("(c p) t -> p c t", p=P)[:, dc,
                                                                    tb * FBT:(tb + 1) * FBT]
                    dma(GP, s_sp, o_ap, Bf[:, dc, 0:FBT])

        SY.run(lambda e, v=s_sp.n: e.wait_ge(s_sp.h, v))
        SY.run(lambda e, v=s_x.n: e.wait_ge(s_x.h, v))
        GP.run(lambda e, v=s_g.n: e.wait_ge(s_g.h, v))

        @block.sync
        def _(e):
            for fn in SY.steps:
                fn(e)

        @block.gpsimd
        def _(e):
            for fn in GP.steps:
                fn(e)

        @block.scalar
        def _(e):
            for fn in SC.steps:
                fn(e)

        @block.vector
        def _(e):
            for fn in VE.steps:
                fn(e)

        @block.tensor
        def _(e):
            for fn in TE.steps:
                fn(e)

    return nc


def _hadamard(n):
    h = np.array([[1.0]], np.float32)
    while h.shape[0] < n:
        h = np.block([[h, h], [h, -h]]).astype(np.float32)
    return h


_NC_CACHE = {}


def kernel(L0_attn_x, L0_attn_ln_scale, L0_attn_ln_bias, L0_attn_mix_k,
           L0_attn_mix_v, L0_attn_mix_r, L0_attn_time_first, L0_attn_time_decay,
           L0_attn_Wk, L0_attn_Wv, L0_attn_Wr, L0_attn_Wo,
           L0_ffn_W1, L0_ffn_W2, L0_ffn_W3):
    X = np.asarray(L0_attn_x, np.float32)

    def pack_vec(v):
        return np.ascontiguousarray(np.asarray(v, np.float32).reshape(NCC, P).T)

    vecs = np.ascontiguousarray(np.stack(
        [pack_vec(L0_attn_ln_scale), pack_vec(L0_attn_ln_bias),
         pack_vec(L0_attn_mix_k), pack_vec(L0_attn_mix_v),
         pack_vec(L0_attn_mix_r), pack_vec(L0_attn_time_first),
         pack_vec(L0_attn_time_decay)], axis=1))

    def tile_sq(W):
        a = np.asarray(W, np.float32).reshape(NCC, P, NCC, P)
        return np.ascontiguousarray(a.transpose(2, 1, 0, 3))

    def tile_w13(W):
        a = np.asarray(W, np.float32).reshape(NH, P, NCC, P)
        return np.ascontiguousarray(a.transpose(0, 3, 2, 1))

    def tile_w2(W):
        a = np.asarray(W, np.float32).reshape(NCC, P, NH, P)
        return np.ascontiguousarray(a.transpose(2, 3, 0, 1))

    wk = tile_sq(L0_attn_Wk); wv = tile_sq(L0_attn_Wv)
    wr = tile_sq(L0_attn_Wr); wo = tile_sq(L0_attn_Wo)
    w1 = tile_w13(L0_ffn_W1); w3 = tile_w13(L0_ffn_W3)
    w2 = tile_w2(L0_ffn_W2)
    h128 = _hadamard(128)
    h64 = _hadamard(64)
    ones = np.ones((P, 1), np.float32)
    onesr = np.ones((1, P), np.float32)

    in_maps = []
    for c in range(8):
        b, pos = divmod(c, 4)
        t0 = pos * TREAL
        xT = np.zeros((2048, HALO + TREAL), np.float32)
        src = X[b, max(0, t0 - HALO):t0 + TREAL]
        xT[:, (HALO + TREAL) - src.shape[0]:] = src.T
        pm = np.full((P, 1), 0.0 if pos == 0 else 1.0, np.float32)
        in_maps.append(dict(
            xT=np.ascontiguousarray(xT), vecs=vecs, pmask=pm, ones=ones,
            onesr=onesr, h128=h128, h64=h64, wk=wk, wv=wv, wr=wr, wo=wo,
            w1=w1, w3=w3, w2=w2))

    if "nc" not in _NC_CACHE:
        _NC_CACHE["nc"] = build_program()
    nc = _NC_CACHE["nc"]
    res = run_bass_kernel_spmd(nc, in_maps, core_ids=list(range(8)))

    out = np.empty((2, 8192, 2048), np.float32)
    for c in range(8):
        b, pos = divmod(c, 4)
        out[b, pos * TREAL:(pos + 1) * TREAL] = np.asarray(res.results[c]["outT"]).T
    return out
